# version 09
# baseline (speedup 1.0000x reference)
import math
import numpy as np
import ml_dtypes

import concourse.bass as bass
import concourse.mybir as mybir
from concourse.bass_utils import run_bass_kernel_spmd

F32 = mybir.dt.float32
BF16 = mybir.dt.bfloat16
AF = mybir.ActivationFunctionType
ALU = mybir.AluOpType
AX = mybir.AxisListType

B = 8
T = 2048
D = 4096
HD = 128
HFF = 11008
HFFP = 11264
HSL = HFFP // 8
NHC = HFFP // 128
SCALE = float(1.0 / math.sqrt(HD))
EPS = 1e-5

NF4 = np.array([-1.0, -0.6961928009986877, -0.5250730514526367,
                -0.39491748809814453, -0.28444138169288635, -0.18477343022823334,
                -0.09105003625154495, 0.0, 0.07958029955625534, 0.16093020141124725,
                0.24611230194568634, 0.33791524171829224, 0.44070982933044434,
                0.5626170039176941, 0.7229568362236023, 1.0], dtype=np.float32)
NF4_BOUNDS = ((NF4[:-1] + NF4[1:]) * 0.5).astype(np.float32)
NF4_DELTA = (NF4[1:].astype(np.float64) - NF4[:-1].astype(np.float64)).astype(np.float32)
NF4_C0 = float(NF4[0])

ENGS = ("tensor", "vector", "scalar", "gpsimd", "sync")


class Sem:
    __slots__ = ("h", "count", "_cm")

    def __init__(self, nc, name):
        self._cm = nc.semaphore(name)
        self.h = self._cm.__enter__()
        self.count = 0

    def bump(self, n=1):
        self.count += n
        return self.count


class Q:

    def __init__(self, plan, eng):
        object.__setattr__(self, "_plan", plan)
        object.__setattr__(self, "_eng", eng)

    def __getattr__(self, name):
        ops = self._plan.ops[self._eng]

        def rec(*a, _inc=None, _incn=1, **kw):
            ops.append((name, a, kw, _inc, _incn))
        return rec


class Plan:
    def __init__(self, nc):
        self.nc = nc
        self.ops = {e: [] for e in ENGS}
        self.q = {e: Q(self, e) for e in ENGS}
        self._sems = []
        self.bar = None
        self.dma_sems = []

    def sem(self, name, is_dma=False):
        s = Sem(self.nc, name)
        self._sems.append(s)
        if is_dma:
            self.dma_sems.append(s)
        return s

    def barrier(self):
        for s in self.dma_sems:
            if s.count:
                self.ops["sync"].append(("wait_ge", (s.h, s.count), {}, None, 1))
        tgt = self.bar.count + len(ENGS)
        self.bar.count = tgt
        for e in ENGS:
            self.ops[e].append(("sem_inc", (self.bar.h, 1), {}, None, 1))
            self.ops[e].append(("wait_ge", (self.bar.h, tgt), {}, None, 1))

    def replay(self, eng_name, eng):
        pend = None
        for (name, a, kw, inc, incn) in self.ops[eng_name]:
            if name == "sem_inc" and pend is not None:
                pend.then_inc(a[0], a[1])
                pend = None
                continue
            r = getattr(eng, name)(*a, **kw)
            if name in ("wait_ge", "sem_inc"):
                pend = None
                continue
            if inc is not None:
                r.then_inc(inc.h, incn)
                pend = None
            else:
                pend = r


class Scope:
    def __init__(self, nc):
        self.nc = nc
        self.cms = []

    def sb(self, name, shape, dtype):
        cm = self.nc.sbuf_tensor(name, shape, dtype)
        h = cm.__enter__()
        self.cms.append(cm)
        return h

    def close(self):
        for cm in reversed(self.cms):
            cm.__exit__(None, None, None)
        self.cms = []



def _llama3_inv_freq():
    ROPE_THETA, ROPE_FACTOR, HIGH_FF, LOW_FF, OMP = 500000.0, 8.0, 4.0, 1.0, 8192
    inv = 1.0 / ROPE_THETA ** (np.arange(0, HD, 2, dtype=np.float32) / HD)
    wavelen = 2.0 * np.pi / inv
    inv_scaled = np.where(wavelen > OMP / LOW_FF, inv / ROPE_FACTOR, inv)
    smooth = (OMP / wavelen - LOW_FF) / (HIGH_FF - LOW_FF)
    smoothed = (1.0 - smooth) / ROPE_FACTOR * inv + smooth * inv
    is_medium = (wavelen <= OMP / LOW_FF) & (wavelen >= OMP / HIGH_FF)
    return np.where(is_medium, smoothed, inv_scaled).astype(np.float32)


def make_tables():
    inv_freq = _llama3_inv_freq()
    pos = np.arange(T, dtype=np.float32)
    freqs = pos[:, None] * inv_freq[None, :]
    emb = np.concatenate([freqs, freqs], axis=-1)
    cosT = np.ascontiguousarray(np.cos(emb).astype(np.float32).T)
    sinT = np.ascontiguousarray(np.sin(emb).astype(np.float32).T)
    sinTs = sinT.copy()
    sinTs[:64, :] *= -1.0
    f = np.arange(512)[None, :]
    p = np.arange(128)[:, None]
    masks = np.stack([(f - p >= d).astype(np.float32) for d in (0, 128, 256, 384)])
    masks_bf = np.ascontiguousarray(
        masks.astype(ml_dtypes.bfloat16).transpose(1, 0, 2).reshape(128, 4 * 512))
    return cosT, sinTs, masks_bf


def pack_qkv(Wq, Wk, Wv):
    WqT = np.ascontiguousarray(Wq.T)
    WkT = np.ascontiguousarray(Wk.T)
    WvT = np.ascontiguousarray(Wv.T)
    out = np.empty((8, 3, 32, 128, 256), dtype=np.float32)
    for g in range(8):
        q = WqT[:, g * 512:(g + 1) * 512]
        k = WkT[:, g * 128:(g + 1) * 128]
        v = WvT[:, g * 128:(g + 1) * 128]
        cols = [q[:, 0:256], q[:, 256:512], np.concatenate([k, v], axis=1)]
        for op in range(3):
            out[g, op] = cols[op].reshape(32, 128, 256)
    return out



def build_program(dbg=False):
    nc = bass.Bass("TRN2", target_bir_lowering=False, detect_race_conditions=False)
    DK = {"kind": "ExternalOutput"} if dbg else {}

    d_xT = nc.dram_tensor("xT", [D, T], F32, kind="ExternalInput").ap()
    d_x = nc.dram_tensor("x", [T, D], F32, kind="ExternalInput").ap()
    d_wqkv = nc.dram_tensor("wqkv", [8, 3, 32, 128, 256], F32, kind="ExternalInput").ap()
    d_woT = nc.dram_tensor("woT", [D, D], F32, kind="ExternalInput").ap()
    d_w1s = nc.dram_tensor("w1s", [HSL, D], F32, kind="ExternalInput").ap()
    d_w2s = nc.dram_tensor("w2s", [HSL, D], F32, kind="ExternalInput").ap()
    d_w3s = nc.dram_tensor("w3s", [D, HSL], F32, kind="ExternalInput").ap()
    d_cosT = nc.dram_tensor("cosT", [HD, T], F32, kind="ExternalInput").ap()
    d_sinTs = nc.dram_tensor("sinTs", [HD, T], F32, kind="ExternalInput").ap()
    d_masks = nc.dram_tensor("masks", [128, 4 * 512], BF16, kind="ExternalInput").ap()
    d_identf = nc.dram_tensor("identf", [128, 128], F32, kind="ExternalInput").ap()
    d_out = nc.dram_tensor("out", [T, D], F32, kind="ExternalOutput").ap()

    d_qkvT = nc.dram_tensor("qkvT", [8, 5, 128, T], BF16, **DK).ap()
    d_vT = nc.dram_tensor("vT", [8, 128, T], F32, **DK).ap()
    d_dq1s = nc.dram_tensor("dq1s", [HSL, D], BF16).ap()
    d_dq2s = nc.dram_tensor("dq2s", [HSL, D], BF16).ap()
    d_dq3s = nc.dram_tensor("dq3s", [D, HSL], BF16).ap()
    d_dq1g = nc.dram_tensor("dq1g", [HFFP, D], BF16).ap()
    d_dq2gc = [nc.dram_tensor("dq2g%d" % i, [8 * 256, D], BF16).ap() for i in range(5)]
    d_dq2gc.append(nc.dram_tensor("dq2g5", [8 * 128, D], BF16).ap())
    d_dq3g = nc.dram_tensor("dq3g", [8 * D, HSL], BF16).ap()
    d_attnT = nc.dram_tensor("attnT", [D, T], BF16, **DK).ap()
    d_z1 = nc.dram_tensor("z1", [T, D], F32, **DK).ap()
    d_yf = nc.dram_tensor("yf", [T, D], F32, **DK).ap()
    d_yb = nc.dram_tensor("yb", [T, D], BF16, **DK).ap()
    d_hact = nc.dram_tensor("hact", [NHC, 128, T], BF16, **DK).ap()
    d_z2 = nc.dram_tensor("z2", [T, D], F32, **DK).ap()
    if dbg:
        d_dbg1 = nc.dram_tensor("dbg1", [HFFP, D], BF16, kind="ExternalOutput").ap()
        d_dbgs = nc.dram_tensor("dbgs", [HSL, D], BF16, kind="ExternalOutput").ap()
        d_dbgy = nc.dram_tensor("dbgy", [128, 32 * T], BF16, kind="ExternalOutput").ap()
        d_dbgg = nc.dram_tensor("dbgg", [128, 2 * 32 * 256], BF16, kind="ExternalOutput").ap()

    plan = Plan(nc)
    plan.bar = plan.sem("bar")
    TEN, DVE, ACT, PL, SP = (plan.q[e] for e in ENGS)

    perm = Scope(nc)
    s_onesb = perm.sb("s_onesb", [128, 1], BF16)
    s_onesr = perm.sb("s_onesr", [1, 128], F32)
    s_stats = perm.sb("s_stats", [128, 64], F32)
    s_sm = perm.sb("s_sm", [128, 32], F32)

    pbc = [nc.psum_tensor("pb%d" % i, [128, 512], F32) for i in range(8)]
    pb = [cm.__enter__() for cm in pbc]

    S = plan.sem
    sm_tab = S("sm_tab", True)
    sm_xt = S("sm_xt", True)
    sm_wq = [S("sm_wq0", True), S("sm_wq1", True)]
    sm_rot = S("sm_rot", True)
    sm_qkst = S("sm_qkst", True)
    sm_dqin = S("sm_dqin", True)
    sm_dqst = [S("sm_dqst1", True), S("sm_dqst2", True), S("sm_dqst3", True)]
    sm_cc1 = S("sm_cc1")
    sm_cc2 = [S("sm_cc2_%d" % i) for i in range(6)]
    sm_cc3 = S("sm_cc3")
    sm_gld = [S("sm_gld0", True), S("sm_gld1", True)]
    sm_attst = S("sm_attst", True)
    sm_wo = S("sm_wo", True)
    sm_attld = [S("sm_attld0", True), S("sm_attld1", True)]
    sm_xld = [S("sm_xld0", True), S("sm_xld1", True)]
    sm_z1st = S("sm_z1st", True)
    sm_z1ld = [S("sm_z1ld0", True), S("sm_z1ld1", True)]
    sm_yst = S("sm_yst", True)
    sm_yt = S("sm_yt", True)
    sm_gw = [S("sm_gw0", True), S("sm_gw1", True)]
    sm_hast = S("sm_hast", True)
    sm_w3b = [S("sm_w3b0", True), S("sm_w3b1", True)]
    sm_ha = [S("sm_ha0", True), S("sm_ha1", True)]
    sm_yf = [S("sm_yf0", True), S("sm_yf1", True)]
    sm_z2st = S("sm_z2st", True)
    sm_z2ld = [S("sm_z2ld0", True), S("sm_z2ld1", True)]
    sm_out = S("sm_out", True)

    se_pe = S("se_pe")
    se_act = S("se_act")
    se_dve = S("se_dve")
    se_pool = S("se_pool")

    def dma(q, out, in_, sem, transpose=False):
        if transpose:
            q.dma_start(out=out, in_=in_, transpose=True, _inc=sem, _incn=16)
        else:
            q.dma_start(out=out, in_=in_, _inc=sem, _incn=16)
        return sem.bump(16)

    bank_free = [(None, 0)] * 8

    def pe_bank(b):
        sem, cnt = bank_free[b]
        if sem is not None:
            TEN.wait_ge(sem.h, cnt)

    def consumed(b, sem, cnt):
        bank_free[b] = (sem, cnt)

    def dve_fence():
        DVE.sem_inc(se_dve.h, 1)
        DVE.wait_ge(se_dve.h, se_dve.bump())

    b1 = Scope(nc)
    s_cosT = b1.sb("s_cosT", [HD, T], F32)
    s_sinTs = b1.sb("s_sinTs", [HD, T], F32)
    s_xTb = b1.sb("s_xTb", [128, 32 * T], BF16)
    s_op = [b1.sb("s_op0", [128, 8192], BF16), b1.sb("s_op1", [128, 8192], BF16)]
    s_qfp = [b1.sb("s_qf", [128, T], F32), b1.sb("s_qf2", [128, T], F32)]
    s_qr = b1.sb("s_qr", [128, T // 2], F32)
    s_qst = b1.sb("s_qst", [128, T // 2], BF16)
    s_dqi = [b1.sb("s_dqi0", [128, HSL // 2], F32),
             b1.sb("s_dqi1", [128, HSL // 2], F32)]
    s_dqa = b1.sb("s_dqa", [128, HSL // 2], BF16)
    s_dqt = b1.sb("s_dqt", [128, HSL // 2], BF16)
    s_dqm = b1.sb("s_dqm", [128, 32], F32)
    s_dqmr = b1.sb("s_dqmr", [128, 32], F32)

    dma(SP, s_cosT[:, :], d_cosT[:, :], sm_tab)
    dma(SP, s_sinTs[:, :], d_sinTs[:, :], sm_tab)
    tab_t = sm_tab.count
    DVE.memset(s_onesb[:, 0:1], 1.0)
    DVE.memset(s_onesr[0:1, :], 1.0)
    DVE.sem_inc(se_dve.h, 1)
    ones_t = se_dve.bump()

    for i in range(8):
        src = d_xT.rearrange("(c p) t -> p c t", p=128)[:, 4 * i:4 * i + 4, :]
        dst = s_xTb[:, 4 * i * T:(4 * i + 4) * T].rearrange("p (c t) -> p c t", t=T)
        dma(PL, dst, src, sm_xt)
    xt_t = sm_xt.count

    def op_slot(g, op):
        return (3 * g + op) % 2

    def op_tile(slot, hc, sub):
        base = hc * 256 + sub * 128
        return s_op[slot][:, base:base + 128]

    wq_t = {}

    def qkv_prefetch(g, op):
        slot = op_slot(g, op)
        src = d_wqkv[g, op].rearrange("h p o -> p h o")
        dst = s_op[slot].rearrange("p (h o) -> p h o", o=256)
        for k in range(4):
            dma(PL, dst[:, 8 * k:8 * k + 8, :], src[:, 8 * k:8 * k + 8, :], sm_wq[slot])
        wq_t[(g, op)] = sm_wq[slot].count

    dq_jobs = []
    for ti in range(D // 128):
        dq_jobs.append((2, ti, 0, HSL // 2))
        dq_jobs.append((2, ti, HSL // 2, HSL // 2))
    for ti in range(HSL // 128):
        dq_jobs.append((0, ti, 0, 2048))
        dq_jobs.append((0, ti, 2048, 2048))
    W2_CHUNK_TILES = [(0, 2), (2, 4), (4, 6), (6, 8), (8, 10), (10, 11)]
    w2_chunk_job_hi = []
    nw2 = 0
    for ci, (ta, tb) in enumerate(W2_CHUNK_TILES):
        for ti in range(ta, tb):
            if ci < 3:
                dq_jobs.append((1, ti, 0, 2048))
                dq_jobs.append((1, ti, 2048, 2048))
                nw2 += 2
            else:
                for cq in range(8):
                    dq_jobs.append((1, ti, cq * 512, 512))
                    nw2 += 1
        w2_chunk_job_hi.append(nw2)
    dq_srcs = [d_w1s, d_w2s, d_w3s]
    dq_dsts = [d_dq1s, d_dq2s, d_dq3s]
    dq_state = {"next": 0, "last_read": 0}
    N_W3_JOBS = 2 * (D // 128)
    N_W1_JOBS = 2 * (HSL // 128)

    def emit_dq(n, bufs):
        s_ip, s_a, s_t, s_m, s_mr = bufs
        for _ in range(n):
            i = dq_state["next"]
            if i >= len(dq_jobs):
                return
            dq_state["next"] = i + 1
            which, ti, c0, Fd = dq_jobs[i]
            nblk = Fd // 64
            ipar = i % 2
            s_i = s_ip[ipar]
            prev = dq_state.get("read_%d" % ipar, 0)
            if prev:
                SP.wait_ge(se_dve.h, prev)
            ld = dma(SP, s_i[:, 0:Fd], dq_srcs[which][ti * 128:(ti + 1) * 128, c0:c0 + Fd],
                     sm_dqin)
            DVE.wait_ge(sm_dqin.h, ld)
            inF = s_i[:, 0:Fd]
            in3 = inF.rearrange("p (g k) -> p g k", k=64)
            am = s_m[:, 0:nblk]
            am3 = am.broadcast_to((128, nblk, 64))
            amr = s_mr[:, 0:nblk]
            amr3 = amr.broadcast_to((128, nblk, 64))
            nF = inF
            acc = s_a[:, 0:Fd]
            tmp = s_t[:, 0:Fd]
            DVE.tensor_reduce(out=am, in_=in3, axis=AX.X, op=ALU.max,
                              apply_absolute_value=True)
            dve_fence()
            DVE.tensor_scalar(out=s_a[:, 0:nblk], in0=am, scalar1=0.0, scalar2=1.0,
                              op0=ALU.is_equal, op1=ALU.mult)
            dve_fence()
            DVE.tensor_tensor(out=amr, in0=am, in1=s_a[:, 0:nblk], op=ALU.add)
            dve_fence()
            DVE.reciprocal(out=amr, in_=amr)
            dve_fence()
            DVE.tensor_tensor(out=in3, in0=in3, in1=amr3, op=ALU.mult)
            DVE.tensor_scalar(out=acc, in0=nF, scalar1=float(NF4_BOUNDS[0]),
                              scalar2=float(NF4_DELTA[0]), op0=ALU.is_gt, op1=ALU.mult)
            for j in range(1, 15):
                DVE.tensor_scalar(out=tmp, in0=nF, scalar1=float(NF4_BOUNDS[j]),
                                  scalar2=float(NF4_DELTA[j]), op0=ALU.is_gt, op1=ALU.mult)
                DVE.tensor_tensor(out=acc, in0=acc, in1=tmp, op=ALU.add)
            DVE.scalar_tensor_tensor(out=tmp.rearrange("p (g k) -> p g k", k=64),
                                     in0=acc.rearrange("p (g k) -> p g k", k=64),
                                     scalar=NF4_C0, in1=am3, op0=ALU.add, op1=ALU.mult)
            DVE.sem_inc(se_dve.h, 1)
            rd = se_dve.bump()
            dq_state["read_%d" % ipar] = rd
            SP.wait_ge(se_dve.h, rd)
            dma(SP, dq_dsts[which][ti * 128:(ti + 1) * 128, c0:c0 + Fd], tmp[:, 0:Fd],
                sm_dqst[which])

    dq_bufs1 = (s_dqi, s_dqa, s_dqt, s_dqm, s_dqmr)

    qkv_prefetch(0, 0)
    qkv_prefetch(0, 1)
    last_rp_buf = [0, 0]
    last_vst_buf = [0, 0]
    last_qst = 0
    for g in range(8):
        for oc in range(6):
            emit_dq(2 if (g * 6 + oc) % 3 == 2 else 1, dq_bufs1)
            op, sub = divmod(oc, 2)
            slot = op_slot(g, op)
            TEN.wait_ge(sm_wq[slot].h, wq_t[(g, op)])
            if g == 0 and oc == 0:
                TEN.wait_ge(sm_xt.h, xt_t)
            banks = [0, 1, 2, 3] if (oc % 2 == 0) else [4, 5, 6, 7]
            for tc in range(4):
                pe_bank(banks[tc])
            for hc in range(32):
                for tc in range(4):
                    TEN.matmul(pb[banks[tc]][:, :], lhsT=op_tile(slot, hc, sub),
                               rhs=s_xTb[:, hc * T + tc * 512: hc * T + tc * 512 + 512],
                               start=(hc == 0), stop=(hc == 31))
            TEN.sem_inc(se_pe.h, 1)
            mm_t = se_pe.bump()
            if sub == 1:
                nxt = (g, op + 2) if op + 2 < 3 else ((g + 1, op - 1) if g < 7 else None)
                if nxt is not None:
                    PL.wait_ge(se_pe.h, mm_t)
                    qkv_prefetch(*nxt)
            ACT.wait_ge(se_pe.h, mm_t)
            qpar = oc % 2
            qf = s_qfp[qpar]
            if oc < 5:
                if last_rp_buf[qpar]:
                    ACT.wait_ge(se_pool.h, last_rp_buf[qpar])
                if last_vst_buf[qpar]:
                    ACT.wait_ge(sm_qkst.h, last_vst_buf[qpar])
                for tc in range(4):
                    ACT.activation(out=qf[:, tc * 512:(tc + 1) * 512],
                                   in_=pb[banks[tc]][:, :], func=AF.Copy)
                ACT.sem_inc(se_act.h, 1)
                ev_t = se_act.bump()
                for tc in range(4):
                    consumed(banks[tc], se_act, ev_t)
                PL.wait_ge(se_act.h, ev_t)
                PL.wait_ge(sm_tab.h, tab_t)
                H2 = T // 2
                for th in range(2):
                    cs = th * H2
                    qfh = qf[:, cs:cs + H2]
                    dma(PL, s_qr[0:64, :], qf[64:128, cs:cs + H2], sm_rot)
                    rot_t = dma(PL, s_qr[64:128, :], qf[0:64, cs:cs + H2], sm_rot)
                    PL.wait_ge(sm_rot.h, rot_t)
                    PL.tensor_tensor(out=qfh, in0=qfh, in1=s_cosT[:, cs:cs + H2],
                                     op=ALU.mult)
                    PL.tensor_tensor(out=s_qr[:, :], in0=s_qr[:, :],
                                     in1=s_sinTs[:, cs:cs + H2], op=ALU.mult)
                    if last_qst:
                        PL.wait_ge(sm_qkst.h, last_qst)
                    PL.tensor_tensor(out=s_qst[:, :], in0=qfh, in1=s_qr[:, :], op=ALU.add)
                    PL.sem_inc(se_pool.h, 1)
                    last_rp_buf[qpar] = se_pool.bump()
                    SP.wait_ge(se_pool.h, last_rp_buf[qpar])
                    last_qst = dma(SP, d_qkvT[g, oc][:, cs:cs + H2], s_qst[:, :], sm_qkst)
            else:
                if last_vst_buf[qpar]:
                    ACT.wait_ge(sm_qkst.h, last_vst_buf[qpar])
                if last_rp_buf[qpar]:
                    ACT.wait_ge(se_pool.h, last_rp_buf[qpar])
                for tc in range(4):
                    ACT.activation(out=qf[:, tc * 512:(tc + 1) * 512],
                                   in_=pb[banks[tc]][:, :], func=AF.Copy)
                ACT.sem_inc(se_act.h, 1)
                ev_t = se_act.bump()
                for tc in range(4):
                    consumed(banks[tc], se_act, ev_t)
                SP.wait_ge(se_act.h, ev_t)
                dma(SP, d_vT[g][:, 0:1024], qf[:, 0:1024], sm_qkst)
                last_vst_buf[qpar] = dma(SP, d_vT[g][:, 1024:2048], qf[:, 1024:2048], sm_qkst)

    emit_dq(max(0, N_W3_JOBS - dq_state["next"]), dq_bufs1)
    PL.wait_ge(sm_dqst[2].h, 16 * N_W3_JOBS)
    PL.collective_compute("AllGather", ALU.bypass, replica_groups=[list(range(8))],
                          ins=[d_dq3s.opt()], outs=[d_dq3g.opt()], _inc=sm_cc3, _incn=1)

    plan.barrier()
    b1.close()

    b2 = Scope(nc)
    s_masks = b2.sb("s_masks", [128, 4 * 512], BF16)
    s_identf = b2.sb("s_identf", [128, 128], F32)
    s_qkT = [b2.sb("s_qkT0", [128, 5 * T], BF16), b2.sb("s_qkT1", [128, 5 * T], BF16)]
    s_vTf = [b2.sb("s_vTf0", [128, T], F32), b2.sb("s_vTf1", [128, T], F32)]
    s_Vb = b2.sb("s_Vb", [128, T], BF16)
    s_sq = b2.sb("s_sq", [128, T], BF16)
    s_pT = b2.sb("s_pT", [128, 4 * 512], BF16)
    s_att = b2.sb("s_att", [128, 2 * T], BF16)
    s_bc = b2.sb("s_bc", [128, 512], F32)
    s_rcd = b2.sb("s_rcd", [1, 512], F32)
    s_negB = b2.sb("s_negB", [128, 4], F32)
    s_dqi2 = [b2.sb("s_dqi20", [128, 2048], F32),
              b2.sb("s_dqi21", [128, 2048], F32)]
    s_dqa2 = b2.sb("s_dqa2", [128, 2048], BF16)
    s_dqt2 = b2.sb("s_dqt2", [128, 2048], BF16)
    s_dqm2 = b2.sb("s_dqm2", [128, 32], F32)
    s_dqmr2 = b2.sb("s_dqmr2", [128, 32], F32)
    dq_bufs2 = (s_dqi2, s_dqa2, s_dqt2, s_dqm2, s_dqmr2)

    dma(SP, s_masks[:, :], d_masks[:, :], sm_tab)
    dma(SP, s_identf[:, :], d_identf[:, :], sm_tab)
    tab2_t = sm_tab.count

    SP.wait_ge(sm_qkst.h, sm_qkst.count)

    def b2_load_group(g, par):
        for oc in range(5):
            dma(SP, s_qkT[par][:, oc * T:(oc + 1) * T], d_qkvT[g, oc], sm_gld[par])
        dma(SP, s_vTf[par][:, :], d_vT[g], sm_gld[par])
        return sm_gld[par].count

    gld_t = {0: b2_load_group(0, 0)}
    last_bc_mm = 0
    gather1_done = [False]
    gather2_done = [False]

    for g in range(8):
        par = g % 2
        if g < 7:
            gld_t[g + 1] = b2_load_group(g + 1, 1 - par)

        qTg = s_qkT[par]
        kTg = s_qkT[par][:, 4 * T:5 * T]
        DVE.wait_ge(se_act.h, se_act.count)
        DVE.wait_ge(se_pe.h, se_pe.count)
        TEN.wait_ge(sm_gld[par].h, gld_t[g])
        TEN.wait_ge(sm_tab.h, tab2_t)

        vc_t = 0
        for j in range(16):
            bk = 2 + (j % 2)
            pe_bank(bk)
            TEN.transpose(pb[bk][:, 0:128], s_vTf[par][:, j * 128:(j + 1) * 128],
                          s_identf[:, :])
            TEN.sem_inc(se_pe.h, 1)
            t_t = se_pe.bump()
            DVE.wait_ge(se_pe.h, t_t)
            DVE.tensor_copy(out=s_Vb[:, j * 128:(j + 1) * 128], in_=pb[bk][:, 0:128])
            DVE.sem_inc(se_dve.h, 1)
            vc_t = se_dve.bump()
            consumed(bk, se_dve, vc_t)

        last_nmm = 0
        for oc in (4, 0, 1, 2, 3):
            ACT.wait_ge(sm_gld[par].h, gld_t[g])
            if last_nmm:
                ACT.wait_ge(se_pe.h, last_nmm)
            ACT.activation(out=s_sq[:, :], in_=qTg[:, oc * T:(oc + 1) * T], func=AF.Square)
            ACT.sem_inc(se_act.h, 1)
            sq_t = se_act.bump()
            TEN.wait_ge(se_act.h, sq_t)
            TEN.wait_ge(se_dve.h, ones_t)
            mx = s_sm[0:1, oc:oc + 1]
            for tc in range(4):
                bk = 6 + (tc % 2)
                pe_bank(bk)
                TEN.matmul(pb[bk][0:1, :], lhsT=s_onesb[:, 0:1],
                           rhs=s_sq[:, tc * 512:(tc + 1) * 512], start=True, stop=True)
                TEN.sem_inc(se_pe.h, 1)
                last_nmm = se_pe.bump()
                DVE.wait_ge(se_pe.h, last_nmm)
                if tc == 0:
                    DVE.tensor_reduce(out=mx, in_=pb[bk][0:1, :], axis=AX.X, op=ALU.max)
                    dve_fence()
                else:
                    DVE.tensor_reduce(out=s_sm[0:1, 8:9], in_=pb[bk][0:1, :], axis=AX.X,
                                      op=ALU.max)
                    dve_fence()
                    DVE.tensor_tensor(out=mx, in0=mx, in1=s_sm[0:1, 8:9], op=ALU.max)
                    dve_fence()
                DVE.sem_inc(se_dve.h, 1)
                consumed(bk, se_dve, se_dve.bump())
        for h in range(4):
            DVE.tensor_tensor(out=s_sm[0:1, 9:10], in0=s_sm[0:1, h:h + 1],
                              in1=s_sm[0:1, 4:5], op=ALU.mult)
            DVE.sem_inc(se_dve.h, 1)
            p_t = se_dve.bump()
            DVE.wait_ge(se_dve.h, p_t)
            ACT.wait_ge(se_dve.h, p_t)
            ACT.activation(out=s_sm[0:1, 10:11], in_=s_sm[0:1, 9:10], func=AF.Sqrt)
            ACT.sem_inc(se_act.h, 1)
            sq2_t = se_act.bump()
            DVE.wait_ge(se_act.h, sq2_t)
            DVE.tensor_scalar(out=s_sm[0:1, 11:12], in0=s_sm[0:1, 10:11],
                              scalar1=float(-1.02 * SCALE), scalar2=None, op0=ALU.mult)
            DVE.sem_inc(se_dve.h, 1)
            nb_t = se_dve.bump()
            TEN.wait_ge(se_dve.h, nb_t)
            bk = 6 + (h % 2)
            pe_bank(bk)
            TEN.matmul(pb[bk][:, 0:1], lhsT=s_onesr[0:1, :], rhs=s_sm[0:1, 11:12],
                       start=True, stop=True)
            TEN.sem_inc(se_pe.h, 1)
            bb_t = se_pe.bump()
            DVE.wait_ge(se_pe.h, bb_t)
            DVE.tensor_copy(out=s_negB[:, h:h + 1], in_=pb[bk][:, 0:1])
            DVE.sem_inc(se_dve.h, 1)
            consumed(bk, se_dve, se_dve.bump())
        negB_t = se_dve.count

        for hp in range(2):
            h0 = 2 * hp
            DVE.wait_ge(sm_attst.h, sm_attst.count)
            for c in range(4):
                jmax = 4 * c + 3
                chunk = hp * 4 + c
                s_bank = {(0, 0): 4, (0, 1): 4, (1, 0): 5, (1, 1): 5}
                pv_bank = {0: 6, 1: 7} if chunk % 2 == 0 else {0: 2, 1: 3}
                dn_bank = {0: 0, 1: 1}
                for b_ in (0, 1, 4, 5, pv_bank[0], pv_bank[1]):
                    pe_bank(b_)
                exp_t = {}
                msk_t = {}

                def emit_s_exp(j):
                    parj = j % 2
                    for hh in range(2):
                        sb = s_bank[(hh, parj)]
                        sem, cnt = bank_free[sb]
                        if sem is not None:
                            TEN.wait_ge(sem.h, cnt)
                        TEN.matmul(pb[sb][:, :],
                                   lhsT=kTg[:, j * 128:(j + 1) * 128],
                                   rhs=qTg[:, (h0 + hh) * T + c * 512:
                                           (h0 + hh) * T + c * 512 + 512],
                                   start=True, stop=True)
                    TEN.sem_inc(se_pe.h, 1)
                    s_t = se_pe.bump()
                    ACT.wait_ge(se_pe.h, s_t)
                    if j == 0:
                        ACT.wait_ge(se_dve.h, negB_t)
                    for hh in range(2):
                        sb = s_bank[(hh, parj)]
                        pt = s_pT[:, (hh * 2 + parj) * 512:(hh * 2 + parj) * 512 + 512]
                        ACT.activation(out=pt, in_=pb[sb][:, :], func=AF.Exp,
                                       bias=s_negB[:, h0 + hh:h0 + hh + 1], scale=SCALE)
                        ACT.sem_inc(se_act.h, 1)
                        e_t = se_act.bump()
                        consumed(sb, se_act, e_t)
                        exp_t[(hh, j)] = e_t
                        if j >= 4 * c:
                            di = j - 4 * c
                            PL.wait_ge(se_act.h, e_t)
                            PL.tensor_tensor(out=pt, in0=pt,
                                             in1=s_masks[:, di * 512:(di + 1) * 512],
                                             op=ALU.mult)
                            PL.sem_inc(se_pool.h, 1)
                            msk_t[(hh, j)] = se_pool.bump()

                def emit_pv(j):
                    parj = j % 2
                    for hh in range(2):
                        pt = s_pT[:, (hh * 2 + parj) * 512:(hh * 2 + parj) * 512 + 512]
                        if (hh, j) in msk_t:
                            TEN.wait_ge(se_pool.h, msk_t[(hh, j)])
                        else:
                            TEN.wait_ge(se_act.h, exp_t[(hh, j)])
                        if j == 0:
                            TEN.wait_ge(se_dve.h, vc_t)
                        TEN.matmul(pb[pv_bank[hh]][:, :],
                                   lhsT=s_Vb[:, j * 128:(j + 1) * 128],
                                   rhs=pt, start=(j == 0), stop=(j == jmax))
                        TEN.matmul(pb[dn_bank[hh]][0:1, :], lhsT=s_onesb[:, 0:1],
                                   rhs=pt, start=(j == 0), stop=(j == jmax))
                        if j == jmax:
                            TEN.sem_inc(se_pe.h, 1)
                            se_pe.bump()

                emit_s_exp(0)
                for j in range(1, jmax + 1):
                    emit_s_exp(j)
                    emit_pv(j - 1)
                emit_pv(jmax)
                fin_t = se_pe.count
                if dq_state["next"] < N_W3_JOBS + N_W1_JOBS:
                    emit_dq(1, dq_bufs2)
                if not gather1_done[0] and dq_state["next"] >= N_W3_JOBS + N_W1_JOBS:
                    gather1_done[0] = True
                    PL.wait_ge(sm_dqst[0].h, 16 * N_W1_JOBS)
                    PL.collective_compute("AllGather", ALU.bypass,
                                          replica_groups=[list(range(8))],
                                          ins=[d_dq1s.opt()], outs=[d_dq1g.opt()],
                                          _inc=sm_cc1, _incn=1)
                for hh in range(2):
                    h = h0 + hh
                    DVE.wait_ge(se_pe.h, fin_t)
                    if last_bc_mm:
                        DVE.wait_ge(se_pe.h, last_bc_mm)
                    DVE.reciprocal(out=s_rcd[0:1, :], in_=pb[dn_bank[hh]][0:1, :])
                    DVE.sem_inc(se_dve.h, 1)
                    rc_t = se_dve.bump()
                    DVE.wait_ge(se_dve.h, rc_t)
                    consumed(dn_bank[hh], se_dve, rc_t)
                    TEN.wait_ge(se_dve.h, rc_t)
                    bk = (2 if chunk % 2 == 0 else 6) + hh
                    pe_bank(bk)
                    TEN.matmul(pb[bk][:, :], lhsT=s_onesr[0:1, :], rhs=s_rcd[0:1, :],
                               start=True, stop=True)
                    TEN.sem_inc(se_pe.h, 1)
                    last_bc_mm = se_pe.bump()
                    DVE.wait_ge(se_pe.h, last_bc_mm)
                    DVE.tensor_copy(out=s_bc[:, :], in_=pb[bk][:, :])
                    DVE.sem_inc(se_dve.h, 1)
                    bce_t = se_dve.bump()
                    DVE.wait_ge(se_dve.h, bce_t)
                    consumed(bk, se_dve, bce_t)
                    DVE.tensor_tensor(
                        out=s_att[:, hh * T + c * 512: hh * T + c * 512 + 512],
                        in0=pb[pv_bank[hh]][:, :], in1=s_bc[:, :], op=ALU.mult)
                    DVE.sem_inc(se_dve.h, 1)
                    pv_t = se_dve.bump()
                    consumed(pv_bank[hh], se_dve, pv_t)
            SP.wait_ge(se_dve.h, se_dve.count)
            for hh in range(2):
                head = 4 * g + h0 + hh
                dma(SP, d_attnT[head * 128:(head + 1) * 128, :],
                    s_att[:, hh * T:(hh + 1) * T], sm_attst)

    emit_dq(max(0, N_W3_JOBS + N_W1_JOBS - dq_state["next"]), dq_bufs2)
    if not gather1_done[0]:
        PL.wait_ge(sm_dqst[0].h, 16 * N_W1_JOBS)
        PL.collective_compute("AllGather", ALU.bypass, replica_groups=[list(range(8))],
                              ins=[d_dq1s.opt()], outs=[d_dq1g.opt()], _inc=sm_cc1, _incn=1)

    plan.barrier()
    b2.close()

    c = Scope(nc)
    s_woT = [c.sb("s_woT0", [128, 32 * 1024], BF16),
             c.sb("s_woT1", [128, 32 * 1024], BF16)]
    s_atb = [c.sb("s_atb0", [128, 4 * 256], BF16), c.sb("s_atb1", [128, 4 * 256], BF16)]
    s_xt1 = [c.sb("s_xt10", [128, 1024], F32), c.sb("s_xt11", [128, 1024], F32)]
    s_z1s = [c.sb("s_z1s0", [128, 1024], F32), c.sb("s_z1s1", [128, 1024], F32)]
    s_sqs = c.sb("s_sqs", [128, 1024], F32)
    s_dqiC = [c.sb("s_dqiC0", [128, 2048], F32), c.sb("s_dqiC1", [128, 2048], F32)]
    s_dqaC = c.sb("s_dqaC", [128, 2048], BF16)
    s_dqtC = c.sb("s_dqtC", [128, 2048], BF16)
    s_dqmC = c.sb("s_dqmC", [128, 32], F32)
    s_dqmrC = c.sb("s_dqmrC", [128, 32], F32)
    dq_bufs_c = (s_dqiC, s_dqaC, s_dqtC, s_dqmC, s_dqmrC)
    N_PRE_W2 = N_W3_JOBS + N_W1_JOBS
    w2_gather_issued = [False] * 6
    W2_CHUNK_ROWS = [256] * 5 + [128]

    def issue_w2_gathers():
        done = dq_state["next"] - N_PRE_W2
        for ci in range(6):
            if w2_gather_issued[ci] or done < w2_chunk_job_hi[ci]:
                continue
            w2_gather_issued[ci] = True
            PL.wait_ge(sm_dqst[1].h, 16 * w2_chunk_job_hi[ci])
            PL.collective_compute(
                "AllGather", ALU.bypass, replica_groups=[list(range(8))],
                ins=[d_dq2s[256 * ci:256 * ci + W2_CHUNK_ROWS[ci], :].opt()],
                outs=[d_dq2gc[ci].opt()], _inc=sm_cc2[ci], _incn=1)

    SP.wait_ge(sm_attst.h, sm_attst.count)
    ACT.wait_ge(sm_attst.h, sm_attst.count)

    def load_woq(q, wpar):
        for ocn in range(32):
            dma(PL, s_woT[wpar][:, ocn * 1024:(ocn + 1) * 1024],
                d_woT[ocn * 128:(ocn + 1) * 128, q * 1024:(q + 1) * 1024], sm_wo)
        return sm_wo.count

    wo_t = {0: load_woq(0, 0)}
    wo_last_use = [0, 0]
    atb_use = [0, 0]
    last_z1 = {0: 0, 1: 0}
    last_z1st = {0: 0, 1: 0}
    last_ssq_rd = 0
    for q in range(4):
        wpar = q % 2
        if q + 1 < 4:
            if wo_last_use[1 - wpar]:
                PL.wait_ge(se_pe.h, wo_last_use[1 - wpar])
            wo_t[q + 1] = load_woq(q + 1, 1 - wpar)
        for t2 in range(8):
            tpar = t2 % 2
            xld = {}
            for sub in range(2):
                t128 = t2 * 2 + sub
                parx = t128 % 2
                if last_z1[parx]:
                    PL.wait_ge(se_dve.h, last_z1[parx])
                xld[sub] = dma(PL, s_xt1[parx][:, :],
                               d_x[t128 * 128:(t128 + 1) * 128,
                                   q * 1024:(q + 1) * 1024], sm_xld[parx])

            def atb_load(ob, t2i):
                if atb_use[ob % 2]:
                    ACT.wait_ge(se_pe.h, atb_use[ob % 2])
                src_ap = d_attnT.rearrange("(c p) t -> p c t", p=128)[
                    :, 4 * ob:4 * ob + 4, t2i * 256:(t2i + 1) * 256]
                dst_ap = s_atb[ob % 2].rearrange("p (c t) -> p c t", t=256)
                return dma(ACT, dst_ap, src_ap, sm_attld[ob % 2])

            at_t = {0: atb_load(0, t2), 1: atb_load(1, t2)}
            bank0 = tpar * 4
            for b_ in range(bank0, bank0 + 4):
                pe_bank(b_)
            TEN.wait_ge(sm_wo.h, wo_t[q])
            for ob in range(8):
                parb = ob % 2
                TEN.wait_ge(sm_attld[parb].h, at_t[ob])
                for ocl in range(4):
                    ocn = 4 * ob + ocl
                    for sub in range(2):
                        for ch in range(2):
                            TEN.matmul(pb[bank0 + sub * 2 + ch][:, :],
                                       lhsT=s_atb[parb][:, ocl * 256 + sub * 128:
                                                        ocl * 256 + (sub + 1) * 128],
                                       rhs=s_woT[wpar][:, ocn * 1024 + ch * 512:
                                                       ocn * 1024 + (ch + 1) * 512],
                                       start=(ocn == 0), stop=(ocn == 31))
                TEN.sem_inc(se_pe.h, 1)
                mm_oc = se_pe.bump()
                atb_use[ob % 2] = mm_oc
                if ob + 2 < 8:
                    at_t[ob + 2] = atb_load(ob + 2, t2)
            wo_mm_t = se_pe.count
            wo_last_use[wpar] = wo_mm_t
            for sub in range(2):
                t128 = t2 * 2 + sub
                parx = t128 % 2
                DVE.wait_ge(se_pe.h, wo_mm_t)
                DVE.wait_ge(sm_xld[parx].h, xld[sub])
                if last_z1st[parx]:
                    DVE.wait_ge(sm_z1st.h, last_z1st[parx])
                for ch in range(2):
                    DVE.tensor_tensor(out=s_z1s[parx][:, ch * 512:(ch + 1) * 512],
                                      in0=pb[bank0 + sub * 2 + ch][:, :],
                                      in1=s_xt1[parx][:, ch * 512:(ch + 1) * 512],
                                      op=ALU.add)
                DVE.sem_inc(se_dve.h, 1)
                z1_t = se_dve.bump()
                last_z1[parx] = z1_t
                for ch in range(2):
                    consumed(bank0 + sub * 2 + ch, se_dve, z1_t)
                DVE.tensor_reduce(out=s_sm[:, 12:13], in_=s_z1s[parx][:, :], axis=AX.X,
                                  op=ALU.add)
                dve_fence()
                if q == 0:
                    DVE.tensor_copy(out=s_stats[:, t128:t128 + 1], in_=s_sm[:, 12:13])
                else:
                    DVE.tensor_tensor(out=s_stats[:, t128:t128 + 1],
                                      in0=s_stats[:, t128:t128 + 1], in1=s_sm[:, 12:13],
                                      op=ALU.add)
                DVE.sem_inc(se_dve.h, 1)
                se_dve.bump()
                ACT.wait_ge(se_dve.h, z1_t)
                if last_ssq_rd:
                    ACT.wait_ge(se_dve.h, last_ssq_rd)
                ACT.activation(out=s_sqs[:, :], in_=s_z1s[parx][:, :], func=AF.Square,
                               accum_out=s_sm[:, 13:14])
                ACT.sem_inc(se_act.h, 1)
                ssq_t = se_act.bump()
                DVE.wait_ge(se_act.h, ssq_t)
                if q == 0:
                    DVE.tensor_copy(out=s_stats[:, 16 + t128:17 + t128], in_=s_sm[:, 13:14])
                else:
                    DVE.tensor_tensor(out=s_stats[:, 16 + t128:17 + t128],
                                      in0=s_stats[:, 16 + t128:17 + t128],
                                      in1=s_sm[:, 13:14], op=ALU.add)
                DVE.sem_inc(se_dve.h, 1)
                last_ssq_rd = se_dve.bump()
                SP.wait_ge(se_dve.h, z1_t)
                last_z1st[parx] = dma(SP, d_z1[t128 * 128:(t128 + 1) * 128,
                                              q * 1024:(q + 1) * 1024],
                                      s_z1s[parx][:, :], sm_z1st)
                c_sub = (q * 8 + t2) * 2 + sub
                if c_sub % 5 == 0 and dq_state["next"] < N_PRE_W2 + w2_chunk_job_hi[2]:
                    emit_dq(1, dq_bufs_c)
                    issue_w2_gathers()

    emit_dq(max(0, N_PRE_W2 + w2_chunk_job_hi[2] - dq_state["next"]), dq_bufs_c)
    issue_w2_gathers()

    plan.barrier()
    c.close()

    c2 = Scope(nc)
    s_z1l = [c2.sb("s_z1l0", [128, D], F32), c2.sb("s_z1l1", [128, D], F32)]
    s_yst = c2.sb("s_yst", [128, D], F32)
    s_ybs = c2.sb("s_ybs", [128, D], BF16)

    SP.wait_ge(sm_z1st.h, sm_z1st.count)
    z1ld = {}
    for t128 in range(2):
        z1ld[t128] = dma(SP, s_z1l[t128 % 2][:, :],
                         d_z1[t128 * 128:(t128 + 1) * 128, :], sm_z1ld[t128 % 2])
    last_yst = 0
    for t128 in range(16):
        parz = t128 % 2
        DVE.tensor_scalar(out=s_sm[:, 14:15], in0=s_stats[:, t128:t128 + 1],
                          scalar1=float(1.0 / D), scalar2=None, op0=ALU.mult)
        dve_fence()
        DVE.tensor_scalar(out=s_sm[:, 15:16], in0=s_stats[:, 16 + t128:17 + t128],
                          scalar1=float(1.0 / D), scalar2=None, op0=ALU.mult)
        dve_fence()
        DVE.tensor_tensor(out=s_sm[:, 16:17], in0=s_sm[:, 14:15], in1=s_sm[:, 14:15],
                          op=ALU.mult)
        dve_fence()
        DVE.tensor_tensor(out=s_sm[:, 15:16], in0=s_sm[:, 15:16], in1=s_sm[:, 16:17],
                          op=ALU.subtract)
        dve_fence()
        DVE.tensor_scalar(out=s_sm[:, 15:16], in0=s_sm[:, 15:16], scalar1=EPS,
                          scalar2=None, op0=ALU.add)
        DVE.sem_inc(se_dve.h, 1)
        var_t = se_dve.bump()
        ACT.wait_ge(se_dve.h, var_t)
        ACT.activation(out=s_sm[:, 17:18], in_=s_sm[:, 15:16], func=AF.Sqrt)
        ACT.sem_inc(se_act.h, 1)
        sd_t = se_act.bump()
        DVE.wait_ge(se_act.h, sd_t)
        DVE.reciprocal(out=s_sm[:, 18:19], in_=s_sm[:, 17:18])
        dve_fence()
        DVE.wait_ge(sm_z1ld[parz].h, z1ld[t128])
        if last_yst:
            DVE.wait_ge(sm_yst.h, last_yst)
        DVE.tensor_scalar(out=s_yst[:, :], in0=s_z1l[parz][:, :],
                          scalar1=s_sm[:, 14:15], scalar2=s_sm[:, 18:19],
                          op0=ALU.subtract, op1=ALU.mult)
        DVE.tensor_scalar(out=s_ybs[:, :], in0=s_z1l[parz][:, :],
                          scalar1=s_sm[:, 14:15], scalar2=s_sm[:, 18:19],
                          op0=ALU.subtract, op1=ALU.mult)
        DVE.sem_inc(se_dve.h, 1)
        y_t = se_dve.bump()
        if t128 + 2 < 16:
            SP.wait_ge(se_dve.h, y_t)
            z1ld[t128 + 2] = dma(SP, s_z1l[parz][:, :],
                                 d_z1[(t128 + 2) * 128:(t128 + 3) * 128, :], sm_z1ld[parz])
        SP.wait_ge(se_dve.h, y_t)
        dma(SP, d_yf[t128 * 128:(t128 + 1) * 128, :], s_yst[:, :], sm_yst)
        last_yst = dma(SP, d_yb[t128 * 128:(t128 + 1) * 128, :], s_ybs[:, :], sm_yst)

    plan.barrier()
    c2.close()

    dph = Scope(nc)
    s_yTb = dph.sb("s_yTb", [128, 32 * T], BF16)
    s_gsl = [dph.sb("s_gsl0", [128, 2 * 32 * 256], BF16),
             dph.sb("s_gsl1", [128, 2 * 32 * 256], BF16)]
    s_sil = dph.sb("s_sil", [128, T], BF16)
    s_has = dph.sb("s_has", [128, T], BF16)

    SP.wait_ge(sm_yst.h, sm_yst.count)
    SP.wait_ge(sm_cc1.h, 1)
    if dbg:
        for r8 in range(8):
            dma(SP, d_dbg1[r8 * HSL:(r8 + 1) * HSL, :],
                d_dq1g[r8 * HSL:(r8 + 1) * HSL, :], sm_out)
        dma(SP, d_dbgs[:, :], d_dq1s[:, :], sm_out)
    for hc in range(32):
        dma(SP, s_yTb[:, hc * T:(hc + 1) * T],
            d_yb[0:T, hc * 128:(hc + 1) * 128], sm_yt, transpose=True)
    yt_t = sm_yt.count

    def yT(hc, tc):
        return s_yTb[:, hc * T + tc * 512: hc * T + tc * 512 + 512]

    def gsl_tile(par, w, hc, su2):
        base = w * 32 * 256 + hc * 256 + su2 * 128
        return s_gsl[par][:, base:base + 128]

    s_dqiD = [dph.sb("s_dqiD0", [128, 512], F32), dph.sb("s_dqiD1", [128, 512], F32)]
    s_dqaD = dph.sb("s_dqaD", [128, 512], BF16)
    s_dqtD = dph.sb("s_dqtD", [128, 512], BF16)
    s_dqmD = dph.sb("s_dqmD", [128, 16], F32)
    s_dqmrD = dph.sb("s_dqmrD", [128, 16], F32)
    dq_bufs_d = (s_dqiD, s_dqaD, s_dqtD, s_dqmD, s_dqmrD)

    blocks = [(k, r) for k in range(6) for r in range(8)]

    def load_block(b, par):
        k, r = blocks[b]
        sz = 256 if k < 5 else 128
        SP.wait_ge(sm_cc2[k].h, 1)
        for w in range(2):
            for hc in range(32):
                if w == 0:
                    srcb = d_dq1g[1408 * r + 256 * k:1408 * r + 256 * k + sz,
                                  hc * 128:(hc + 1) * 128]
                else:
                    srcb = d_dq2gc[k][sz * r:sz * (r + 1), hc * 128:(hc + 1) * 128]
                dma(SP, s_gsl[par][:, w * 32 * 256 + hc * 256:
                                   w * 32 * 256 + hc * 256 + sz],
                    srcb, sm_gw[par], transpose=True)
        return sm_gw[par].count

    gw_t = {0: load_block(0, 0)}
    if dbg:
        SP.wait_ge(sm_yt.h, sm_yt.count)
        dma(SP, d_dbgy[:, :], s_yTb[:, :], sm_out)
    gsl_last_use = [0, 0]
    last_hast = 0
    n_su2 = 0
    for b in range(len(blocks)):
        k, r = blocks[b]
        par = b % 2
        if b + 1 < len(blocks):
            if gsl_last_use[1 - par]:
                SP.wait_ge(se_pe.h, gsl_last_use[1 - par])
            gw_t[b + 1] = load_block(b + 1, 1 - par)
        nsu = 2 if k < 5 else 1
        for su2 in range(nsu):
            Hc = 11 * r + 2 * k + su2
            TEN.wait_ge(sm_gw[par].h, gw_t[b])
            if b == 0 and su2 == 0:
                TEN.wait_ge(sm_yt.h, yt_t)
            for w in range(2):
                for tc in range(4):
                    pe_bank(w * 4 + tc)
                for hc in range(32):
                    for tc in range(4):
                        TEN.matmul(pb[w * 4 + tc][:, :], lhsT=gsl_tile(par, w, hc, su2),
                                   rhs=yT(hc, tc), start=(hc == 0), stop=(hc == 31))
                TEN.sem_inc(se_pe.h, 1)
                se_pe.bump()
            g_t = se_pe.count
            gsl_last_use[par] = g_t
            ACT.wait_ge(se_pe.h, g_t)
            for tc in range(4):
                ACT.activation(out=s_sil[:, tc * 512:(tc + 1) * 512], in_=pb[tc][:, :],
                               func=AF.Silu)
            ACT.sem_inc(se_act.h, 1)
            sil_t = se_act.bump()
            for tc in range(4):
                consumed(tc, se_act, sil_t)
            DVE.wait_ge(se_act.h, sil_t)
            if last_hast:
                DVE.wait_ge(sm_hast.h, last_hast)
            for tc in range(4):
                DVE.tensor_tensor(out=s_has[:, tc * 512:(tc + 1) * 512],
                                  in0=s_sil[:, tc * 512:(tc + 1) * 512],
                                  in1=pb[4 + tc][:, :], op=ALU.mult)
            DVE.sem_inc(se_dve.h, 1)
            ha_t = se_dve.bump()
            for tc in range(4):
                consumed(4 + tc, se_dve, ha_t)
            SP.wait_ge(se_dve.h, ha_t)
            last_hast = dma(SP, d_hact[Hc], s_has[:, :], sm_hast)
            n_su2 += 1
            if dq_state["next"] < len(dq_jobs):
                emit_dq(1, dq_bufs_d)
                issue_w2_gathers()

    plan.barrier()
    dph.close()

    e = Scope(nc)
    s_w3b = [e.sb("s_w3b0", [128, NHC * 512], BF16),
             e.sb("s_w3b1", [128, NHC * 512], BF16)]
    s_hab = [e.sb("s_hab0", [128, 8 * 512], BF16), e.sb("s_hab1", [128, 8 * 512], BF16)]
    s_yfb = [e.sb("s_yfb%d" % i, [128, 512], F32) for i in range(4)]
    s_z2s = [e.sb("s_z2s0", [128, 512], F32), e.sb("s_z2s1", [128, 512], F32)]

    SP.wait_ge(sm_hast.h, sm_hast.count)
    ACT.wait_ge(sm_hast.h, sm_hast.count)
    PL.wait_ge(sm_yst.h, sm_yst.count)
    SP.wait_ge(sm_cc3.h, 1)
    NB = NHC // 8

    def load_w3(e8, wpar):
        for Hc in range(NHC):
            r, lc = divmod(Hc, HSL // 128)
            dma(SP, s_w3b[wpar][:, Hc * 512:(Hc + 1) * 512],
                d_dq3g[r * D + e8 * 512: r * D + (e8 + 1) * 512, lc * 128:(lc + 1) * 128],
                sm_w3b[wpar], transpose=True)
        return sm_w3b[wpar].count

    w3t = {0: load_w3(0, 0)}
    w3_last_use = [0, 0]
    last_yfl = {0: 0, 1: 0, 2: 0, 3: 0}
    last_z2st = {0: 0, 1: 0}
    hab_use = [0, 0]
    for e8 in range(8):
        wpar = e8 % 2
        if e8 + 1 < 8:
            if w3_last_use[1 - wpar]:
                SP.wait_ge(se_pe.h, w3_last_use[1 - wpar])
            w3t[e8 + 1] = load_w3(e8 + 1, 1 - wpar)
        for t4 in range(4):
            yld = {}
            for sub in range(4):
                t128 = t4 * 4 + sub
                if last_yfl[sub]:
                    PL.wait_ge(se_dve.h, last_yfl[sub])
                yld[sub] = dma(PL, s_yfb[sub][:, :],
                               d_yf[t128 * 128:(t128 + 1) * 128,
                                    e8 * 512:(e8 + 1) * 512], sm_yf[sub % 2])
            ha_tgt = {}
            for hb in range(2):
                if hab_use[hb]:
                    ACT.wait_ge(se_pe.h, hab_use[hb])
                src_ap = d_hact[hb * 8:(hb + 1) * 8].rearrange("c p t -> p c t")[
                    :, :, t4 * 512:(t4 + 1) * 512]
                dst_ap = s_hab[hb].rearrange("p (c t) -> p c t", t=512)
                ha_tgt[hb] = dma(ACT, dst_ap, src_ap, sm_ha[hb])
            for sub in range(4):
                pe_bank(sub)
            TEN.wait_ge(sm_w3b[wpar].h, w3t[e8])
            for hb in range(NB):
                hpar = hb % 2
                TEN.wait_ge(sm_ha[hpar].h, ha_tgt[hb])
                for hcl in range(8):
                    Hc = hb * 8 + hcl
                    for sub in range(4):
                        TEN.matmul(pb[sub][:, :],
                                   lhsT=s_hab[hpar][:, hcl * 512 + sub * 128:
                                                    hcl * 512 + (sub + 1) * 128],
                                   rhs=s_w3b[wpar][:, Hc * 512:(Hc + 1) * 512],
                                   start=(Hc == 0), stop=(Hc == NHC - 1))
                TEN.sem_inc(se_pe.h, 1)
                bt = se_pe.bump()
                hab_use[hpar] = bt
                if hb + 2 < NB:
                    ACT.wait_ge(se_pe.h, bt)
                    src_ap = d_hact[(hb + 2) * 8:(hb + 3) * 8].rearrange(
                        "c p t -> p c t")[:, :, t4 * 512:(t4 + 1) * 512]
                    dst_ap = s_hab[hpar].rearrange("p (c t) -> p c t", t=512)
                    ha_tgt[hb + 2] = dma(ACT, dst_ap, src_ap, sm_ha[hpar])
            dn_t = se_pe.count
            w3_last_use[wpar] = dn_t
            for sub in range(4):
                t128 = t4 * 4 + sub
                zpar = sub % 2
                DVE.wait_ge(se_pe.h, dn_t)
                DVE.wait_ge(sm_yf[sub % 2].h, yld[sub])
                if last_z2st[zpar]:
                    DVE.wait_ge(sm_z2st.h, last_z2st[zpar])
                DVE.tensor_tensor(out=s_z2s[zpar][:, :], in0=pb[sub][:, :],
                                  in1=s_yfb[sub][:, :], op=ALU.add)
                DVE.sem_inc(se_dve.h, 1)
                z2_t = se_dve.bump()
                last_yfl[sub] = z2_t
                consumed(sub, se_dve, z2_t)
                PL.wait_ge(se_dve.h, z2_t)
                last_z2st[zpar] = dma(PL, d_z2[t128 * 128:(t128 + 1) * 128,
                                              e8 * 512:(e8 + 1) * 512],
                                      s_z2s[zpar][:, :], sm_z2st)

    plan.barrier()
    e.close()

    f = Scope(nc)
    s_z2l = [f.sb("s_z2l0", [128, D], F32), f.sb("s_z2l1", [128, D], F32)]
    s_os = f.sb("s_os", [128, D], F32)

    SP.wait_ge(sm_z2st.h, sm_z2st.count)
    z2ld = {}
    for t128 in range(2):
        z2ld[t128] = dma(SP, s_z2l[t128 % 2][:, :],
                         d_z2[t128 * 128:(t128 + 1) * 128, :], sm_z2ld[t128 % 2])
    last_ost = 0
    for t128 in range(16):
        parz = t128 % 2
        DVE.wait_ge(sm_z2ld[parz].h, z2ld[t128])
        DVE.tensor_reduce(out=s_sm[:, 20:21], in_=s_z2l[parz][:, :], axis=AX.X, op=ALU.add)
        dve_fence()
        DVE.tensor_scalar(out=s_sm[:, 20:21], in0=s_sm[:, 20:21], scalar1=float(1.0 / D),
                          scalar2=None, op0=ALU.mult)
        DVE.sem_inc(se_dve.h, 1)
        mn_t = se_dve.bump()
        ACT.wait_ge(sm_z2ld[parz].h, z2ld[t128])
        ACT.wait_ge(se_dve.h, mn_t)
        if last_ost:
            ACT.wait_ge(sm_out.h, last_ost)
        ACT.activation(out=s_os[:, :], in_=s_z2l[parz][:, :], func=AF.Square,
                       bias=s_sm[:, 20:21], scale=-1.0, accum_out=s_sm[:, 21:22])
        ACT.sem_inc(se_act.h, 1)
        sq_t = se_act.bump()
        DVE.wait_ge(se_act.h, sq_t)
        DVE.tensor_scalar(out=s_sm[:, 22:23], in0=s_sm[:, 21:22], scalar1=float(1.0 / D),
                          scalar2=EPS, op0=ALU.mult, op1=ALU.add)
        DVE.sem_inc(se_dve.h, 1)
        vr_t = se_dve.bump()
        ACT.wait_ge(se_dve.h, vr_t)
        ACT.activation(out=s_sm[:, 23:24], in_=s_sm[:, 22:23], func=AF.Sqrt)
        ACT.sem_inc(se_act.h, 1)
        sd_t = se_act.bump()
        DVE.wait_ge(se_act.h, sd_t)
        DVE.reciprocal(out=s_sm[:, 24:25], in_=s_sm[:, 23:24])
        dve_fence()
        DVE.tensor_scalar(out=s_os[:, :], in0=s_z2l[parz][:, :],
                          scalar1=s_sm[:, 20:21], scalar2=s_sm[:, 24:25],
                          op0=ALU.subtract, op1=ALU.mult)
        DVE.sem_inc(se_dve.h, 1)
        o_t = se_dve.bump()
        if t128 + 2 < 16:
            SP.wait_ge(se_dve.h, o_t)
            z2ld[t128 + 2] = dma(SP, s_z2l[parz][:, :],
                                 d_z2[(t128 + 2) * 128:(t128 + 3) * 128, :], sm_z2ld[parz])
        SP.wait_ge(se_dve.h, o_t)
        last_ost = dma(SP, d_out[t128 * 128:(t128 + 1) * 128, :], s_os[:, :], sm_out)

    SP.wait_ge(sm_out.h, sm_out.count)

    with nc.Block() as block:
        @block.tensor
        def _(eng):
            plan.replay("tensor", eng)

        @block.vector
        def _(eng):
            plan.replay("vector", eng)

        @block.scalar
        def _(eng):
            plan.replay("scalar", eng)

        @block.gpsimd
        def _(eng):
            plan.replay("gpsimd", eng)

        @block.sync
        def _(eng):
            plan.replay("sync", eng)

    f.close()
    for cm in reversed(pbc):
        cm.__exit__(None, None, None)
    perm.close()
    return nc


_CACHE = {}


def kernel(L0_attn_x, L0_attn_Wq, L0_attn_Wk, L0_attn_Wv, L0_attn_Wo,
           L0_ffn_w1, L0_ffn_w2, L0_ffn_w3):
    x = np.asarray(L0_attn_x, dtype=np.float32)
    Wq = np.asarray(L0_attn_Wq, dtype=np.float32)
    Wk = np.asarray(L0_attn_Wk, dtype=np.float32)
    Wv = np.asarray(L0_attn_Wv, dtype=np.float32)
    Wo = np.asarray(L0_attn_Wo, dtype=np.float32)
    w1 = np.asarray(L0_ffn_w1, dtype=np.float32)
    w2 = np.asarray(L0_ffn_w2, dtype=np.float32)
    w3 = np.asarray(L0_ffn_w3, dtype=np.float32)

    if "nc" not in _CACHE:
        _CACHE["nc"] = build_program()
    nc = _CACHE["nc"]

    cosT, sinTs, masks_bf = make_tables()
    wqkv = pack_qkv(Wq, Wk, Wv)
    woT = np.ascontiguousarray(Wo.T)
    identf = np.eye(128, dtype=np.float32)

    w1p = np.zeros((HFFP, D), dtype=np.float32)
    w1p[:HFF] = w1
    w2p = np.zeros((HFFP, D), dtype=np.float32)
    w2p[:HFF] = w2
    w3p = np.zeros((D, HFFP), dtype=np.float32)
    w3p[:, :HFF] = w3

    in_maps = []
    for ci in range(B):
        xc = np.ascontiguousarray(x[ci])
        in_maps.append({
            "xT": np.ascontiguousarray(xc.T),
            "x": xc,
            "wqkv": wqkv,
            "woT": woT,
            "w1s": np.ascontiguousarray(w1p[ci * HSL:(ci + 1) * HSL]),
            "w2s": np.ascontiguousarray(w2p[ci * HSL:(ci + 1) * HSL]),
            "w3s": np.ascontiguousarray(w3p[:, ci * HSL:(ci + 1) * HSL]),
            "cosT": cosT,
            "sinTs": sinTs,
            "masks": masks_bf,
            "identf": identf,
        })
    res = run_bass_kernel_spmd(nc, in_maps, core_ids=list(range(B)))
    out = np.stack([np.asarray(r["out"], dtype=np.float32) for r in res.results])
    return out


# version 10
# speedup vs baseline: 1.0586x; 1.0586x over previous
import math
import numpy as np
import ml_dtypes

import concourse.bass as bass
import concourse.mybir as mybir
from concourse.bass_utils import run_bass_kernel_spmd

F32 = mybir.dt.float32
BF16 = mybir.dt.bfloat16
AF = mybir.ActivationFunctionType
ALU = mybir.AluOpType
AX = mybir.AxisListType

B = 8
T = 2048
D = 4096
HD = 128
HFF = 11008
HFFP = 11264
HSL = HFFP // 8
NHC = HFFP // 128
SCALE = float(1.0 / math.sqrt(HD))
EPS = 1e-5

NF4 = np.array([-1.0, -0.6961928009986877, -0.5250730514526367,
                -0.39491748809814453, -0.28444138169288635, -0.18477343022823334,
                -0.09105003625154495, 0.0, 0.07958029955625534, 0.16093020141124725,
                0.24611230194568634, 0.33791524171829224, 0.44070982933044434,
                0.5626170039176941, 0.7229568362236023, 1.0], dtype=np.float32)
NF4_BOUNDS = ((NF4[:-1] + NF4[1:]) * 0.5).astype(np.float32)
NF4_DELTA = (NF4[1:].astype(np.float64) - NF4[:-1].astype(np.float64)).astype(np.float32)
NF4_C0 = float(NF4[0])

ENGS = ("tensor", "vector", "scalar", "gpsimd", "sync")


class Sem:
    __slots__ = ("h", "count", "_cm")

    def __init__(self, nc, name):
        self._cm = nc.semaphore(name)
        self.h = self._cm.__enter__()
        self.count = 0

    def bump(self, n=1):
        self.count += n
        return self.count


class Q:

    def __init__(self, plan, eng):
        object.__setattr__(self, "_plan", plan)
        object.__setattr__(self, "_eng", eng)

    def __getattr__(self, name):
        ops = self._plan.ops[self._eng]

        def rec(*a, _inc=None, _incn=1, **kw):
            ops.append((name, a, kw, _inc, _incn))
        return rec


class Plan:
    def __init__(self, nc):
        self.nc = nc
        self.ops = {e: [] for e in ENGS}
        self.q = {e: Q(self, e) for e in ENGS}
        self._sems = []
        self.bar = None
        self.dma_sems = []

    def sem(self, name, is_dma=False):
        s = Sem(self.nc, name)
        self._sems.append(s)
        if is_dma:
            self.dma_sems.append(s)
        return s

    def barrier(self):
        for s in self.dma_sems:
            if s.count:
                self.ops["sync"].append(("wait_ge", (s.h, s.count), {}, None, 1))
        tgt = self.bar.count + len(ENGS)
        self.bar.count = tgt
        for e in ENGS:
            self.ops[e].append(("sem_inc", (self.bar.h, 1), {}, None, 1))
            self.ops[e].append(("wait_ge", (self.bar.h, tgt), {}, None, 1))

    def replay(self, eng_name, eng):
        pend = None
        for (name, a, kw, inc, incn) in self.ops[eng_name]:
            if name == "sem_inc" and pend is not None:
                pend.then_inc(a[0], a[1])
                pend = None
                continue
            r = getattr(eng, name)(*a, **kw)
            if name in ("wait_ge", "sem_inc"):
                pend = None
                continue
            if inc is not None:
                r.then_inc(inc.h, incn)
                pend = None
            else:
                pend = r


class Scope:
    def __init__(self, nc):
        self.nc = nc
        self.cms = []

    def sb(self, name, shape, dtype):
        cm = self.nc.sbuf_tensor(name, shape, dtype)
        h = cm.__enter__()
        self.cms.append(cm)
        return h

    def close(self):
        for cm in reversed(self.cms):
            cm.__exit__(None, None, None)
        self.cms = []



def _llama3_inv_freq():
    ROPE_THETA, ROPE_FACTOR, HIGH_FF, LOW_FF, OMP = 500000.0, 8.0, 4.0, 1.0, 8192
    inv = 1.0 / ROPE_THETA ** (np.arange(0, HD, 2, dtype=np.float32) / HD)
    wavelen = 2.0 * np.pi / inv
    inv_scaled = np.where(wavelen > OMP / LOW_FF, inv / ROPE_FACTOR, inv)
    smooth = (OMP / wavelen - LOW_FF) / (HIGH_FF - LOW_FF)
    smoothed = (1.0 - smooth) / ROPE_FACTOR * inv + smooth * inv
    is_medium = (wavelen <= OMP / LOW_FF) & (wavelen >= OMP / HIGH_FF)
    return np.where(is_medium, smoothed, inv_scaled).astype(np.float32)


def make_tables():
    inv_freq = _llama3_inv_freq()
    pos = np.arange(T, dtype=np.float32)
    freqs = pos[:, None] * inv_freq[None, :]
    emb = np.concatenate([freqs, freqs], axis=-1)
    cosT = np.ascontiguousarray(np.cos(emb).astype(np.float32).T)
    sinT = np.ascontiguousarray(np.sin(emb).astype(np.float32).T)
    sinTs = sinT.copy()
    sinTs[:64, :] *= -1.0
    f = np.arange(512)[None, :]
    p = np.arange(128)[:, None]
    masks = np.stack([(f - p >= d).astype(np.float32) for d in (0, 128, 256, 384)])
    masks_bf = np.ascontiguousarray(
        masks.astype(ml_dtypes.bfloat16).transpose(1, 0, 2).reshape(128, 4 * 512))
    return cosT, sinTs, masks_bf


def pack_qkv(Wq, Wk, Wv):
    WqT = np.ascontiguousarray(Wq.T)
    WkT = np.ascontiguousarray(Wk.T)
    WvT = np.ascontiguousarray(Wv.T)
    out = np.empty((8, 3, 32, 128, 256), dtype=np.float32)
    for g in range(8):
        q = WqT[:, g * 512:(g + 1) * 512]
        k = WkT[:, g * 128:(g + 1) * 128]
        v = WvT[:, g * 128:(g + 1) * 128]
        cols = [q[:, 0:256], q[:, 256:512], np.concatenate([k, v], axis=1)]
        for op in range(3):
            out[g, op] = cols[op].reshape(32, 128, 256)
    return out



def build_program(dbg=False):
    nc = bass.Bass("TRN2", target_bir_lowering=False, detect_race_conditions=False)
    DK = {"kind": "ExternalOutput"} if dbg else {}

    d_xT = nc.dram_tensor("xT", [D, T], F32, kind="ExternalInput").ap()
    d_x = nc.dram_tensor("x", [T, D], F32, kind="ExternalInput").ap()
    d_wqkv = nc.dram_tensor("wqkv", [8, 3, 32, 128, 256], F32, kind="ExternalInput").ap()
    d_woT = nc.dram_tensor("woT", [D, D], F32, kind="ExternalInput").ap()
    d_w1s = nc.dram_tensor("w1s", [HSL, D], F32, kind="ExternalInput").ap()
    d_w2s = nc.dram_tensor("w2s", [HSL, D], F32, kind="ExternalInput").ap()
    d_w3s = nc.dram_tensor("w3s", [D, HSL], F32, kind="ExternalInput").ap()
    d_cosT = nc.dram_tensor("cosT", [HD, T], F32, kind="ExternalInput").ap()
    d_sinTs = nc.dram_tensor("sinTs", [HD, T], F32, kind="ExternalInput").ap()
    d_masks = nc.dram_tensor("masks", [128, 4 * 512], BF16, kind="ExternalInput").ap()
    d_identf = nc.dram_tensor("identf", [128, 128], F32, kind="ExternalInput").ap()
    d_out = nc.dram_tensor("out", [T, D], F32, kind="ExternalOutput").ap()

    d_qkvT = nc.dram_tensor("qkvT", [8, 5, 128, T], BF16, **DK).ap()
    d_vT = nc.dram_tensor("vT", [8, 128, T], F32, **DK).ap()
    d_dq1s = nc.dram_tensor("dq1s", [HSL, D], BF16).ap()
    d_dq2s = nc.dram_tensor("dq2s", [HSL, D], BF16).ap()
    d_dq3s = nc.dram_tensor("dq3s", [D, HSL], BF16).ap()
    d_dq1g = nc.dram_tensor("dq1g", [HFFP, D], BF16).ap()
    d_dq2gc = [nc.dram_tensor("dq2g%d" % i, [8 * 256, D], BF16).ap() for i in range(5)]
    d_dq2gc.append(nc.dram_tensor("dq2g5", [8 * 128, D], BF16).ap())
    d_dq3g = nc.dram_tensor("dq3g", [8 * D, HSL], BF16).ap()
    d_attnT = nc.dram_tensor("attnT", [D, T], BF16, **DK).ap()
    d_z1 = nc.dram_tensor("z1", [T, D], F32, **DK).ap()
    d_yf = nc.dram_tensor("yf", [T, D], F32, **DK).ap()
    d_yb = nc.dram_tensor("yb", [T, D], BF16, **DK).ap()
    d_hact = nc.dram_tensor("hact", [NHC, 128, T], BF16, **DK).ap()
    d_z2 = nc.dram_tensor("z2", [T, D], F32, **DK).ap()
    if dbg:
        d_dbg1 = nc.dram_tensor("dbg1", [HFFP, D], BF16, kind="ExternalOutput").ap()
        d_dbgs = nc.dram_tensor("dbgs", [HSL, D], BF16, kind="ExternalOutput").ap()
        d_dbgy = nc.dram_tensor("dbgy", [128, 32 * T], BF16, kind="ExternalOutput").ap()
        d_dbgg = nc.dram_tensor("dbgg", [128, 2 * 32 * 256], BF16, kind="ExternalOutput").ap()

    plan = Plan(nc)
    plan.bar = plan.sem("bar")
    TEN, DVE, ACT, PL, SP = (plan.q[e] for e in ENGS)

    perm = Scope(nc)
    s_onesb = perm.sb("s_onesb", [128, 1], BF16)
    s_onesr = perm.sb("s_onesr", [1, 128], F32)
    s_stats = perm.sb("s_stats", [128, 64], F32)
    s_sm = perm.sb("s_sm", [128, 32], F32)

    pbc = [nc.psum_tensor("pb%d" % i, [128, 512], F32) for i in range(8)]
    pb = [cm.__enter__() for cm in pbc]

    S = plan.sem
    sm_tab = S("sm_tab", True)
    sm_xt = S("sm_xt", True)
    sm_wq = [S("sm_wq0", True), S("sm_wq1", True)]
    sm_rot = S("sm_rot", True)
    sm_qkst = S("sm_qkst", True)
    sm_dqin = S("sm_dqin", True)
    sm_dqst = [S("sm_dqst1", True), S("sm_dqst2", True), S("sm_dqst3", True)]
    sm_cc1 = S("sm_cc1")
    sm_cc2 = [S("sm_cc2_%d" % i) for i in range(6)]
    sm_cc3 = S("sm_cc3")
    sm_gld = [S("sm_gld0", True), S("sm_gld1", True)]
    sm_attst = S("sm_attst", True)
    sm_wo = S("sm_wo", True)
    sm_attld = [S("sm_attld0", True), S("sm_attld1", True)]
    sm_xld = [S("sm_xld0", True), S("sm_xld1", True)]
    sm_z1st = S("sm_z1st", True)
    sm_z1ld = [S("sm_z1ld0", True), S("sm_z1ld1", True)]
    sm_yst = S("sm_yst", True)
    sm_yt = S("sm_yt", True)
    sm_gw = [S("sm_gw0", True), S("sm_gw1", True)]
    sm_hast = S("sm_hast", True)
    sm_w3b = [S("sm_w3b0", True), S("sm_w3b1", True)]
    sm_ha = [S("sm_ha0", True), S("sm_ha1", True)]
    sm_yf = [S("sm_yf0", True), S("sm_yf1", True)]
    sm_z2st = S("sm_z2st", True)
    sm_z2ld = [S("sm_z2ld0", True), S("sm_z2ld1", True)]
    sm_out = S("sm_out", True)

    se_pe = S("se_pe")
    se_act = S("se_act")
    se_dve = S("se_dve")
    se_pool = S("se_pool")

    def dma(q, out, in_, sem, transpose=False):
        if transpose:
            q.dma_start(out=out, in_=in_, transpose=True, _inc=sem, _incn=16)
        else:
            q.dma_start(out=out, in_=in_, _inc=sem, _incn=16)
        return sem.bump(16)

    bank_free = [(None, 0)] * 8

    def pe_bank(b):
        sem, cnt = bank_free[b]
        if sem is not None:
            TEN.wait_ge(sem.h, cnt)

    def consumed(b, sem, cnt):
        bank_free[b] = (sem, cnt)

    def dve_fence():
        DVE.sem_inc(se_dve.h, 1)
        DVE.wait_ge(se_dve.h, se_dve.bump())

    b1 = Scope(nc)
    s_cosT = b1.sb("s_cosT", [HD, T], F32)
    s_sinTs = b1.sb("s_sinTs", [HD, T], F32)
    s_xTb = b1.sb("s_xTb", [128, 32 * T], BF16)
    s_op = [b1.sb("s_op0", [128, 8192], BF16), b1.sb("s_op1", [128, 8192], BF16)]
    s_qfp = [b1.sb("s_qf", [128, T], F32), b1.sb("s_qf2", [128, T], F32)]
    s_qr = b1.sb("s_qr", [128, T // 2], F32)
    s_qst = b1.sb("s_qst", [128, T // 2], BF16)
    s_dqi = [b1.sb("s_dqi0", [128, HSL // 2], F32),
             b1.sb("s_dqi1", [128, HSL // 2], F32)]
    s_dqa = b1.sb("s_dqa", [128, HSL // 2], BF16)
    s_dqt = b1.sb("s_dqt", [128, HSL // 2], BF16)
    s_dqm = b1.sb("s_dqm", [128, 32], F32)
    s_dqmr = b1.sb("s_dqmr", [128, 32], F32)

    dma(SP, s_cosT[:, :], d_cosT[:, :], sm_tab)
    dma(SP, s_sinTs[:, :], d_sinTs[:, :], sm_tab)
    tab_t = sm_tab.count
    DVE.memset(s_onesb[:, 0:1], 1.0)
    DVE.memset(s_onesr[0:1, :], 1.0)
    DVE.sem_inc(se_dve.h, 1)
    ones_t = se_dve.bump()

    for i in range(8):
        src = d_xT.rearrange("(c p) t -> p c t", p=128)[:, 4 * i:4 * i + 4, :]
        dst = s_xTb[:, 4 * i * T:(4 * i + 4) * T].rearrange("p (c t) -> p c t", t=T)
        dma(PL, dst, src, sm_xt)
    xt_t = sm_xt.count

    def op_slot(g, op):
        return (3 * g + op) % 2

    def op_tile(slot, hc, sub):
        base = hc * 256 + sub * 128
        return s_op[slot][:, base:base + 128]

    wq_t = {}

    def qkv_prefetch(g, op):
        slot = op_slot(g, op)
        src = d_wqkv[g, op].rearrange("h p o -> p h o")
        dst = s_op[slot].rearrange("p (h o) -> p h o", o=256)
        for k in range(4):
            dma(PL, dst[:, 8 * k:8 * k + 8, :], src[:, 8 * k:8 * k + 8, :], sm_wq[slot])
        wq_t[(g, op)] = sm_wq[slot].count

    dq_jobs = []
    for ti in range(D // 128):
        dq_jobs.append((2, ti, 0, HSL // 2))
        dq_jobs.append((2, ti, HSL // 2, HSL // 2))
    for ti in range(HSL // 128):
        for cq in range(8):
            dq_jobs.append((0, ti, cq * 512, 512))
    W2_CHUNK_TILES = [(0, 2), (2, 4), (4, 6), (6, 8), (8, 10), (10, 11)]
    w2_chunk_job_hi = []
    nw2 = 0
    for ci, (ta, tb) in enumerate(W2_CHUNK_TILES):
        for ti in range(ta, tb):
            if ci < 3:
                dq_jobs.append((1, ti, 0, 2048))
                dq_jobs.append((1, ti, 2048, 2048))
                nw2 += 2
            else:
                for cq in range(8):
                    dq_jobs.append((1, ti, cq * 512, 512))
                    nw2 += 1
        w2_chunk_job_hi.append(nw2)
    dq_srcs = [d_w1s, d_w2s, d_w3s]
    dq_dsts = [d_dq1s, d_dq2s, d_dq3s]
    dq_state = {"next": 0, "last_read": 0}
    N_W3_JOBS = 2 * (D // 128)
    N_W1_JOBS = 8 * (HSL // 128)

    def emit_dq(n, bufs):
        s_ip, s_a, s_t, s_m, s_mr = bufs
        for _ in range(n):
            i = dq_state["next"]
            if i >= len(dq_jobs):
                return
            dq_state["next"] = i + 1
            which, ti, c0, Fd = dq_jobs[i]
            nblk = Fd // 64
            ipar = i % 2
            s_i = s_ip[ipar]
            prev = dq_state.get("read_%d" % ipar, 0)
            if prev:
                SP.wait_ge(se_dve.h, prev)
            ld = dma(SP, s_i[:, 0:Fd], dq_srcs[which][ti * 128:(ti + 1) * 128, c0:c0 + Fd],
                     sm_dqin)
            DVE.wait_ge(sm_dqin.h, ld)
            inF = s_i[:, 0:Fd]
            in3 = inF.rearrange("p (g k) -> p g k", k=64)
            am = s_m[:, 0:nblk]
            am3 = am.broadcast_to((128, nblk, 64))
            amr = s_mr[:, 0:nblk]
            amr3 = amr.broadcast_to((128, nblk, 64))
            nF = inF
            acc = s_a[:, 0:Fd]
            tmp = s_t[:, 0:Fd]
            DVE.tensor_reduce(out=am, in_=in3, axis=AX.X, op=ALU.max,
                              apply_absolute_value=True)
            dve_fence()
            DVE.tensor_scalar(out=s_a[:, 0:nblk], in0=am, scalar1=0.0, scalar2=1.0,
                              op0=ALU.is_equal, op1=ALU.mult)
            dve_fence()
            DVE.tensor_tensor(out=amr, in0=am, in1=s_a[:, 0:nblk], op=ALU.add)
            dve_fence()
            DVE.reciprocal(out=amr, in_=amr)
            dve_fence()
            DVE.tensor_tensor(out=in3, in0=in3, in1=amr3, op=ALU.mult)
            DVE.tensor_scalar(out=acc, in0=nF, scalar1=float(NF4_BOUNDS[0]),
                              scalar2=float(NF4_DELTA[0]), op0=ALU.is_gt, op1=ALU.mult)
            for j in range(1, 15):
                DVE.tensor_scalar(out=tmp, in0=nF, scalar1=float(NF4_BOUNDS[j]),
                                  scalar2=float(NF4_DELTA[j]), op0=ALU.is_gt, op1=ALU.mult)
                DVE.tensor_tensor(out=acc, in0=acc, in1=tmp, op=ALU.add)
            DVE.scalar_tensor_tensor(out=tmp.rearrange("p (g k) -> p g k", k=64),
                                     in0=acc.rearrange("p (g k) -> p g k", k=64),
                                     scalar=NF4_C0, in1=am3, op0=ALU.add, op1=ALU.mult)
            DVE.sem_inc(se_dve.h, 1)
            rd = se_dve.bump()
            dq_state["read_%d" % ipar] = rd
            SP.wait_ge(se_dve.h, rd)
            dma(SP, dq_dsts[which][ti * 128:(ti + 1) * 128, c0:c0 + Fd], tmp[:, 0:Fd],
                sm_dqst[which])

    dq_bufs1 = (s_dqi, s_dqa, s_dqt, s_dqm, s_dqmr)

    qkv_prefetch(0, 0)
    qkv_prefetch(0, 1)
    last_rp_buf = [0, 0]
    last_vst_buf = [0, 0]
    last_qst = 0
    for g in range(8):
        for oc in range(6):
            slot = g * 6 + oc
            if dq_state["next"] < N_W3_JOBS + N_W1_JOBS // 2:
                emit_dq(3 if slot % 4 == 3 else 2, dq_bufs1)
            op, sub = divmod(oc, 2)
            slot = op_slot(g, op)
            TEN.wait_ge(sm_wq[slot].h, wq_t[(g, op)])
            if g == 0 and oc == 0:
                TEN.wait_ge(sm_xt.h, xt_t)
            banks = [0, 1, 2, 3] if (oc % 2 == 0) else [4, 5, 6, 7]
            for tc in range(4):
                pe_bank(banks[tc])
            for hc in range(32):
                for tc in range(4):
                    TEN.matmul(pb[banks[tc]][:, :], lhsT=op_tile(slot, hc, sub),
                               rhs=s_xTb[:, hc * T + tc * 512: hc * T + tc * 512 + 512],
                               start=(hc == 0), stop=(hc == 31))
            TEN.sem_inc(se_pe.h, 1)
            mm_t = se_pe.bump()
            if sub == 1:
                nxt = (g, op + 2) if op + 2 < 3 else ((g + 1, op - 1) if g < 7 else None)
                if nxt is not None:
                    PL.wait_ge(se_pe.h, mm_t)
                    qkv_prefetch(*nxt)
            ACT.wait_ge(se_pe.h, mm_t)
            qpar = oc % 2
            qf = s_qfp[qpar]
            if oc < 5:
                if last_rp_buf[qpar]:
                    ACT.wait_ge(se_pool.h, last_rp_buf[qpar])
                if last_vst_buf[qpar]:
                    ACT.wait_ge(sm_qkst.h, last_vst_buf[qpar])
                for tc in range(4):
                    ACT.activation(out=qf[:, tc * 512:(tc + 1) * 512],
                                   in_=pb[banks[tc]][:, :], func=AF.Copy)
                ACT.sem_inc(se_act.h, 1)
                ev_t = se_act.bump()
                for tc in range(4):
                    consumed(banks[tc], se_act, ev_t)
                PL.wait_ge(se_act.h, ev_t)
                PL.wait_ge(sm_tab.h, tab_t)
                H2 = T // 2
                for th in range(2):
                    cs = th * H2
                    qfh = qf[:, cs:cs + H2]
                    dma(PL, s_qr[0:64, :], qf[64:128, cs:cs + H2], sm_rot)
                    rot_t = dma(PL, s_qr[64:128, :], qf[0:64, cs:cs + H2], sm_rot)
                    PL.wait_ge(sm_rot.h, rot_t)
                    PL.tensor_tensor(out=qfh, in0=qfh, in1=s_cosT[:, cs:cs + H2],
                                     op=ALU.mult)
                    PL.tensor_tensor(out=s_qr[:, :], in0=s_qr[:, :],
                                     in1=s_sinTs[:, cs:cs + H2], op=ALU.mult)
                    if last_qst:
                        PL.wait_ge(sm_qkst.h, last_qst)
                    PL.tensor_tensor(out=s_qst[:, :], in0=qfh, in1=s_qr[:, :], op=ALU.add)
                    PL.sem_inc(se_pool.h, 1)
                    last_rp_buf[qpar] = se_pool.bump()
                    SP.wait_ge(se_pool.h, last_rp_buf[qpar])
                    last_qst = dma(SP, d_qkvT[g, oc][:, cs:cs + H2], s_qst[:, :], sm_qkst)
            else:
                if last_vst_buf[qpar]:
                    ACT.wait_ge(sm_qkst.h, last_vst_buf[qpar])
                if last_rp_buf[qpar]:
                    ACT.wait_ge(se_pool.h, last_rp_buf[qpar])
                for tc in range(4):
                    ACT.activation(out=qf[:, tc * 512:(tc + 1) * 512],
                                   in_=pb[banks[tc]][:, :], func=AF.Copy)
                ACT.sem_inc(se_act.h, 1)
                ev_t = se_act.bump()
                for tc in range(4):
                    consumed(banks[tc], se_act, ev_t)
                SP.wait_ge(se_act.h, ev_t)
                dma(SP, d_vT[g][:, 0:1024], qf[:, 0:1024], sm_qkst)
                last_vst_buf[qpar] = dma(SP, d_vT[g][:, 1024:2048], qf[:, 1024:2048], sm_qkst)

    if dq_state["next"] < N_W3_JOBS:
        emit_dq(N_W3_JOBS - dq_state["next"], dq_bufs1)
    PL.wait_ge(sm_dqst[2].h, 16 * N_W3_JOBS)
    PL.collective_compute("AllGather", ALU.bypass, replica_groups=[list(range(8))],
                          ins=[d_dq3s.opt()], outs=[d_dq3g.opt()], _inc=sm_cc3, _incn=1)

    plan.barrier()
    b1.close()

    b2 = Scope(nc)
    s_masks = b2.sb("s_masks", [128, 4 * 512], BF16)
    s_identf = b2.sb("s_identf", [128, 128], F32)
    s_qkT = [b2.sb("s_qkT0", [128, 5 * T], BF16), b2.sb("s_qkT1", [128, 5 * T], BF16)]
    s_vTf = [b2.sb("s_vTf0", [128, T], F32), b2.sb("s_vTf1", [128, T], F32)]
    s_Vb = b2.sb("s_Vb", [128, T], BF16)
    s_sq = b2.sb("s_sq", [128, T], BF16)
    s_pT = b2.sb("s_pT", [128, 4 * 512], BF16)
    s_att = b2.sb("s_att", [128, 2 * T], BF16)
    s_bc = b2.sb("s_bc", [128, 512], F32)
    s_rcd = b2.sb("s_rcd", [1, 512], F32)
    s_negB = b2.sb("s_negB", [128, 4], F32)
    s_dqi2 = [b2.sb("s_dqi20", [128, 2048], F32),
              b2.sb("s_dqi21", [128, 2048], F32)]
    s_dqa2 = b2.sb("s_dqa2", [128, 2048], BF16)
    s_dqt2 = b2.sb("s_dqt2", [128, 2048], BF16)
    s_dqm2 = b2.sb("s_dqm2", [128, 32], F32)
    s_dqmr2 = b2.sb("s_dqmr2", [128, 32], F32)
    dq_bufs2 = (s_dqi2, s_dqa2, s_dqt2, s_dqm2, s_dqmr2)

    dma(SP, s_masks[:, :], d_masks[:, :], sm_tab)
    dma(SP, s_identf[:, :], d_identf[:, :], sm_tab)
    tab2_t = sm_tab.count

    SP.wait_ge(sm_qkst.h, sm_qkst.count)

    def b2_load_group(g, par):
        for oc in range(5):
            dma(SP, s_qkT[par][:, oc * T:(oc + 1) * T], d_qkvT[g, oc], sm_gld[par])
        dma(SP, s_vTf[par][:, :], d_vT[g], sm_gld[par])
        return sm_gld[par].count

    gld_t = {0: b2_load_group(0, 0)}
    last_bc_mm = 0
    gather1_done = [False]
    N_PRE_W2 = N_W3_JOBS + N_W1_JOBS
    w2_gather_issued = [False] * 6
    W2_CHUNK_ROWS = [256] * 5 + [128]

    def issue_w2_gathers():
        done = dq_state["next"] - N_PRE_W2
        for ci in range(6):
            if w2_gather_issued[ci] or done < w2_chunk_job_hi[ci]:
                continue
            w2_gather_issued[ci] = True
            PL.wait_ge(sm_dqst[1].h, 16 * w2_chunk_job_hi[ci])
            PL.collective_compute(
                "AllGather", ALU.bypass, replica_groups=[list(range(8))],
                ins=[d_dq2s[256 * ci:256 * ci + W2_CHUNK_ROWS[ci], :].opt()],
                outs=[d_dq2gc[ci].opt()], _inc=sm_cc2[ci], _incn=1)

    for g in range(8):
        par = g % 2
        if g < 7:
            gld_t[g + 1] = b2_load_group(g + 1, 1 - par)

        qTg = s_qkT[par]
        kTg = s_qkT[par][:, 4 * T:5 * T]
        DVE.wait_ge(se_act.h, se_act.count)
        DVE.wait_ge(se_pe.h, se_pe.count)
        TEN.wait_ge(sm_gld[par].h, gld_t[g])
        TEN.wait_ge(sm_tab.h, tab2_t)

        vc_t = 0
        for j in range(16):
            bk = 2 + (j % 2)
            pe_bank(bk)
            TEN.transpose(pb[bk][:, 0:128], s_vTf[par][:, j * 128:(j + 1) * 128],
                          s_identf[:, :])
            TEN.sem_inc(se_pe.h, 1)
            t_t = se_pe.bump()
            DVE.wait_ge(se_pe.h, t_t)
            DVE.tensor_copy(out=s_Vb[:, j * 128:(j + 1) * 128], in_=pb[bk][:, 0:128])
            DVE.sem_inc(se_dve.h, 1)
            vc_t = se_dve.bump()
            consumed(bk, se_dve, vc_t)

        last_nmm = 0
        for oc in (4, 0, 1, 2, 3):
            ACT.wait_ge(sm_gld[par].h, gld_t[g])
            if last_nmm:
                ACT.wait_ge(se_pe.h, last_nmm)
            ACT.activation(out=s_sq[:, :], in_=qTg[:, oc * T:(oc + 1) * T], func=AF.Square)
            ACT.sem_inc(se_act.h, 1)
            sq_t = se_act.bump()
            TEN.wait_ge(se_act.h, sq_t)
            TEN.wait_ge(se_dve.h, ones_t)
            mx = s_sm[0:1, oc:oc + 1]
            for tc in range(4):
                bk = 6 + (tc % 2)
                pe_bank(bk)
                TEN.matmul(pb[bk][0:1, :], lhsT=s_onesb[:, 0:1],
                           rhs=s_sq[:, tc * 512:(tc + 1) * 512], start=True, stop=True)
                TEN.sem_inc(se_pe.h, 1)
                last_nmm = se_pe.bump()
                DVE.wait_ge(se_pe.h, last_nmm)
                if tc == 0:
                    DVE.tensor_reduce(out=mx, in_=pb[bk][0:1, :], axis=AX.X, op=ALU.max)
                    dve_fence()
                else:
                    DVE.tensor_reduce(out=s_sm[0:1, 8:9], in_=pb[bk][0:1, :], axis=AX.X,
                                      op=ALU.max)
                    dve_fence()
                    DVE.tensor_tensor(out=mx, in0=mx, in1=s_sm[0:1, 8:9], op=ALU.max)
                    dve_fence()
                DVE.sem_inc(se_dve.h, 1)
                consumed(bk, se_dve, se_dve.bump())
        for h in range(4):
            DVE.tensor_tensor(out=s_sm[0:1, 9:10], in0=s_sm[0:1, h:h + 1],
                              in1=s_sm[0:1, 4:5], op=ALU.mult)
            DVE.sem_inc(se_dve.h, 1)
            p_t = se_dve.bump()
            DVE.wait_ge(se_dve.h, p_t)
            ACT.wait_ge(se_dve.h, p_t)
            ACT.activation(out=s_sm[0:1, 10:11], in_=s_sm[0:1, 9:10], func=AF.Sqrt)
            ACT.sem_inc(se_act.h, 1)
            sq2_t = se_act.bump()
            DVE.wait_ge(se_act.h, sq2_t)
            DVE.tensor_scalar(out=s_sm[0:1, 11:12], in0=s_sm[0:1, 10:11],
                              scalar1=float(-1.02 * SCALE), scalar2=None, op0=ALU.mult)
            DVE.sem_inc(se_dve.h, 1)
            nb_t = se_dve.bump()
            TEN.wait_ge(se_dve.h, nb_t)
            bk = 6 + (h % 2)
            pe_bank(bk)
            TEN.matmul(pb[bk][:, 0:1], lhsT=s_onesr[0:1, :], rhs=s_sm[0:1, 11:12],
                       start=True, stop=True)
            TEN.sem_inc(se_pe.h, 1)
            bb_t = se_pe.bump()
            DVE.wait_ge(se_pe.h, bb_t)
            DVE.tensor_copy(out=s_negB[:, h:h + 1], in_=pb[bk][:, 0:1])
            DVE.sem_inc(se_dve.h, 1)
            consumed(bk, se_dve, se_dve.bump())
        negB_t = se_dve.count

        for hp in range(2):
            h0 = 2 * hp
            DVE.wait_ge(sm_attst.h, sm_attst.count)
            for c in range(4):
                jmax = 4 * c + 3
                chunk = hp * 4 + c
                s_bank = {(0, 0): 4, (0, 1): 4, (1, 0): 5, (1, 1): 5}
                pv_bank = {0: 6, 1: 7} if chunk % 2 == 0 else {0: 2, 1: 3}
                dn_bank = {0: 0, 1: 1}
                for b_ in (0, 1, 4, 5, pv_bank[0], pv_bank[1]):
                    pe_bank(b_)
                exp_t = {}
                msk_t = {}

                def emit_s_exp(j):
                    parj = j % 2
                    for hh in range(2):
                        sb = s_bank[(hh, parj)]
                        sem, cnt = bank_free[sb]
                        if sem is not None:
                            TEN.wait_ge(sem.h, cnt)
                        TEN.matmul(pb[sb][:, :],
                                   lhsT=kTg[:, j * 128:(j + 1) * 128],
                                   rhs=qTg[:, (h0 + hh) * T + c * 512:
                                           (h0 + hh) * T + c * 512 + 512],
                                   start=True, stop=True)
                    TEN.sem_inc(se_pe.h, 1)
                    s_t = se_pe.bump()
                    ACT.wait_ge(se_pe.h, s_t)
                    if j == 0:
                        ACT.wait_ge(se_dve.h, negB_t)
                    for hh in range(2):
                        sb = s_bank[(hh, parj)]
                        pt = s_pT[:, (hh * 2 + parj) * 512:(hh * 2 + parj) * 512 + 512]
                        ACT.activation(out=pt, in_=pb[sb][:, :], func=AF.Exp,
                                       bias=s_negB[:, h0 + hh:h0 + hh + 1], scale=SCALE)
                        ACT.sem_inc(se_act.h, 1)
                        e_t = se_act.bump()
                        consumed(sb, se_act, e_t)
                        exp_t[(hh, j)] = e_t
                        if j >= 4 * c:
                            di = j - 4 * c
                            PL.wait_ge(se_act.h, e_t)
                            PL.tensor_tensor(out=pt, in0=pt,
                                             in1=s_masks[:, di * 512:(di + 1) * 512],
                                             op=ALU.mult)
                            PL.sem_inc(se_pool.h, 1)
                            msk_t[(hh, j)] = se_pool.bump()

                def emit_pv(j):
                    parj = j % 2
                    for hh in range(2):
                        pt = s_pT[:, (hh * 2 + parj) * 512:(hh * 2 + parj) * 512 + 512]
                        if (hh, j) in msk_t:
                            TEN.wait_ge(se_pool.h, msk_t[(hh, j)])
                        else:
                            TEN.wait_ge(se_act.h, exp_t[(hh, j)])
                        if j == 0:
                            TEN.wait_ge(se_dve.h, vc_t)
                        TEN.matmul(pb[pv_bank[hh]][:, :],
                                   lhsT=s_Vb[:, j * 128:(j + 1) * 128],
                                   rhs=pt, start=(j == 0), stop=(j == jmax))
                        TEN.matmul(pb[dn_bank[hh]][0:1, :], lhsT=s_onesb[:, 0:1],
                                   rhs=pt, start=(j == 0), stop=(j == jmax))
                        if j == jmax:
                            TEN.sem_inc(se_pe.h, 1)
                            se_pe.bump()

                emit_s_exp(0)
                for j in range(1, jmax + 1):
                    emit_s_exp(j)
                    emit_pv(j - 1)
                emit_pv(jmax)
                fin_t = se_pe.count
                if dq_state["next"] < N_PRE_W2 + w2_chunk_job_hi[2]:
                    emit_dq(1, dq_bufs2)
                if not gather1_done[0] and dq_state["next"] >= N_PRE_W2:
                    gather1_done[0] = True
                    PL.wait_ge(sm_dqst[0].h, 16 * N_W1_JOBS)
                    PL.collective_compute("AllGather", ALU.bypass,
                                          replica_groups=[list(range(8))],
                                          ins=[d_dq1s.opt()], outs=[d_dq1g.opt()],
                                          _inc=sm_cc1, _incn=1)
                issue_w2_gathers()
                for hh in range(2):
                    h = h0 + hh
                    DVE.wait_ge(se_pe.h, fin_t)
                    if last_bc_mm:
                        DVE.wait_ge(se_pe.h, last_bc_mm)
                    DVE.reciprocal(out=s_rcd[0:1, :], in_=pb[dn_bank[hh]][0:1, :])
                    DVE.sem_inc(se_dve.h, 1)
                    rc_t = se_dve.bump()
                    DVE.wait_ge(se_dve.h, rc_t)
                    consumed(dn_bank[hh], se_dve, rc_t)
                    TEN.wait_ge(se_dve.h, rc_t)
                    bk = (2 if chunk % 2 == 0 else 6) + hh
                    pe_bank(bk)
                    TEN.matmul(pb[bk][:, :], lhsT=s_onesr[0:1, :], rhs=s_rcd[0:1, :],
                               start=True, stop=True)
                    TEN.sem_inc(se_pe.h, 1)
                    last_bc_mm = se_pe.bump()
                    DVE.wait_ge(se_pe.h, last_bc_mm)
                    DVE.tensor_copy(out=s_bc[:, :], in_=pb[bk][:, :])
                    DVE.sem_inc(se_dve.h, 1)
                    bce_t = se_dve.bump()
                    DVE.wait_ge(se_dve.h, bce_t)
                    consumed(bk, se_dve, bce_t)
                    DVE.tensor_tensor(
                        out=s_att[:, hh * T + c * 512: hh * T + c * 512 + 512],
                        in0=pb[pv_bank[hh]][:, :], in1=s_bc[:, :], op=ALU.mult)
                    DVE.sem_inc(se_dve.h, 1)
                    pv_t = se_dve.bump()
                    consumed(pv_bank[hh], se_dve, pv_t)
            SP.wait_ge(se_dve.h, se_dve.count)
            for hh in range(2):
                head = 4 * g + h0 + hh
                dma(SP, d_attnT[head * 128:(head + 1) * 128, :],
                    s_att[:, hh * T:(hh + 1) * T], sm_attst)

    emit_dq(max(0, N_PRE_W2 + w2_chunk_job_hi[2] - dq_state["next"]), dq_bufs2)
    if not gather1_done[0]:
        PL.wait_ge(sm_dqst[0].h, 16 * N_W1_JOBS)
        PL.collective_compute("AllGather", ALU.bypass, replica_groups=[list(range(8))],
                              ins=[d_dq1s.opt()], outs=[d_dq1g.opt()], _inc=sm_cc1, _incn=1)
    issue_w2_gathers()

    plan.barrier()
    b2.close()

    c = Scope(nc)
    s_woT = [c.sb("s_woT0", [128, 32 * 1024], BF16),
             c.sb("s_woT1", [128, 32 * 1024], BF16)]
    s_atb = [c.sb("s_atb0", [128, 4 * 256], BF16), c.sb("s_atb1", [128, 4 * 256], BF16)]
    s_xt1 = [c.sb("s_xt10", [128, 1024], F32), c.sb("s_xt11", [128, 1024], F32)]
    s_z1s = [c.sb("s_z1s0", [128, 1024], F32), c.sb("s_z1s1", [128, 1024], F32)]
    s_sqs = c.sb("s_sqs", [128, 1024], F32)
    SP.wait_ge(sm_attst.h, sm_attst.count)
    ACT.wait_ge(sm_attst.h, sm_attst.count)

    def load_woq(q, wpar):
        for ocn in range(32):
            dma(PL, s_woT[wpar][:, ocn * 1024:(ocn + 1) * 1024],
                d_woT[ocn * 128:(ocn + 1) * 128, q * 1024:(q + 1) * 1024], sm_wo)
        return sm_wo.count

    wo_t = {0: load_woq(0, 0)}
    wo_last_use = [0, 0]
    atb_use = [0, 0]
    last_z1 = {0: 0, 1: 0}
    last_z1st = {0: 0, 1: 0}
    last_ssq_rd = 0
    for q in range(4):
        wpar = q % 2
        if q + 1 < 4:
            if wo_last_use[1 - wpar]:
                PL.wait_ge(se_pe.h, wo_last_use[1 - wpar])
            wo_t[q + 1] = load_woq(q + 1, 1 - wpar)
        for t2 in range(8):
            tpar = t2 % 2
            xld = {}
            for sub in range(2):
                t128 = t2 * 2 + sub
                parx = t128 % 2
                if last_z1[parx]:
                    PL.wait_ge(se_dve.h, last_z1[parx])
                xld[sub] = dma(PL, s_xt1[parx][:, :],
                               d_x[t128 * 128:(t128 + 1) * 128,
                                   q * 1024:(q + 1) * 1024], sm_xld[parx])

            def atb_load(ob, t2i):
                if atb_use[ob % 2]:
                    ACT.wait_ge(se_pe.h, atb_use[ob % 2])
                src_ap = d_attnT.rearrange("(c p) t -> p c t", p=128)[
                    :, 4 * ob:4 * ob + 4, t2i * 256:(t2i + 1) * 256]
                dst_ap = s_atb[ob % 2].rearrange("p (c t) -> p c t", t=256)
                return dma(ACT, dst_ap, src_ap, sm_attld[ob % 2])

            at_t = {0: atb_load(0, t2), 1: atb_load(1, t2)}
            bank0 = tpar * 4
            for b_ in range(bank0, bank0 + 4):
                pe_bank(b_)
            TEN.wait_ge(sm_wo.h, wo_t[q])
            for ob in range(8):
                parb = ob % 2
                TEN.wait_ge(sm_attld[parb].h, at_t[ob])
                for ocl in range(4):
                    ocn = 4 * ob + ocl
                    for sub in range(2):
                        for ch in range(2):
                            TEN.matmul(pb[bank0 + sub * 2 + ch][:, :],
                                       lhsT=s_atb[parb][:, ocl * 256 + sub * 128:
                                                        ocl * 256 + (sub + 1) * 128],
                                       rhs=s_woT[wpar][:, ocn * 1024 + ch * 512:
                                                       ocn * 1024 + (ch + 1) * 512],
                                       start=(ocn == 0), stop=(ocn == 31))
                TEN.sem_inc(se_pe.h, 1)
                mm_oc = se_pe.bump()
                atb_use[ob % 2] = mm_oc
                if ob + 2 < 8:
                    at_t[ob + 2] = atb_load(ob + 2, t2)
            wo_mm_t = se_pe.count
            wo_last_use[wpar] = wo_mm_t
            for sub in range(2):
                t128 = t2 * 2 + sub
                parx = t128 % 2
                DVE.wait_ge(se_pe.h, wo_mm_t)
                DVE.wait_ge(sm_xld[parx].h, xld[sub])
                if last_z1st[parx]:
                    DVE.wait_ge(sm_z1st.h, last_z1st[parx])
                for ch in range(2):
                    DVE.tensor_tensor(out=s_z1s[parx][:, ch * 512:(ch + 1) * 512],
                                      in0=pb[bank0 + sub * 2 + ch][:, :],
                                      in1=s_xt1[parx][:, ch * 512:(ch + 1) * 512],
                                      op=ALU.add)
                DVE.sem_inc(se_dve.h, 1)
                z1_t = se_dve.bump()
                last_z1[parx] = z1_t
                for ch in range(2):
                    consumed(bank0 + sub * 2 + ch, se_dve, z1_t)
                DVE.tensor_reduce(out=s_sm[:, 12:13], in_=s_z1s[parx][:, :], axis=AX.X,
                                  op=ALU.add)
                dve_fence()
                if q == 0:
                    DVE.tensor_copy(out=s_stats[:, t128:t128 + 1], in_=s_sm[:, 12:13])
                else:
                    DVE.tensor_tensor(out=s_stats[:, t128:t128 + 1],
                                      in0=s_stats[:, t128:t128 + 1], in1=s_sm[:, 12:13],
                                      op=ALU.add)
                DVE.sem_inc(se_dve.h, 1)
                se_dve.bump()
                ACT.wait_ge(se_dve.h, z1_t)
                if last_ssq_rd:
                    ACT.wait_ge(se_dve.h, last_ssq_rd)
                ACT.activation(out=s_sqs[:, :], in_=s_z1s[parx][:, :], func=AF.Square,
                               accum_out=s_sm[:, 13:14])
                ACT.sem_inc(se_act.h, 1)
                ssq_t = se_act.bump()
                DVE.wait_ge(se_act.h, ssq_t)
                if q == 0:
                    DVE.tensor_copy(out=s_stats[:, 16 + t128:17 + t128], in_=s_sm[:, 13:14])
                else:
                    DVE.tensor_tensor(out=s_stats[:, 16 + t128:17 + t128],
                                      in0=s_stats[:, 16 + t128:17 + t128],
                                      in1=s_sm[:, 13:14], op=ALU.add)
                DVE.sem_inc(se_dve.h, 1)
                last_ssq_rd = se_dve.bump()
                SP.wait_ge(se_dve.h, z1_t)
                last_z1st[parx] = dma(SP, d_z1[t128 * 128:(t128 + 1) * 128,
                                              q * 1024:(q + 1) * 1024],
                                      s_z1s[parx][:, :], sm_z1st)

    plan.barrier()
    c.close()

    c2 = Scope(nc)
    s_z1l = [c2.sb("s_z1l0", [128, D], F32), c2.sb("s_z1l1", [128, D], F32)]
    s_yst = c2.sb("s_yst", [128, D], F32)
    s_ybs = c2.sb("s_ybs", [128, D], BF16)

    SP.wait_ge(sm_z1st.h, sm_z1st.count)
    z1ld = {}
    for t128 in range(2):
        z1ld[t128] = dma(SP, s_z1l[t128 % 2][:, :],
                         d_z1[t128 * 128:(t128 + 1) * 128, :], sm_z1ld[t128 % 2])
    last_yst = 0
    for t128 in range(16):
        parz = t128 % 2
        DVE.tensor_scalar(out=s_sm[:, 14:15], in0=s_stats[:, t128:t128 + 1],
                          scalar1=float(1.0 / D), scalar2=None, op0=ALU.mult)
        dve_fence()
        DVE.tensor_scalar(out=s_sm[:, 15:16], in0=s_stats[:, 16 + t128:17 + t128],
                          scalar1=float(1.0 / D), scalar2=None, op0=ALU.mult)
        dve_fence()
        DVE.tensor_tensor(out=s_sm[:, 16:17], in0=s_sm[:, 14:15], in1=s_sm[:, 14:15],
                          op=ALU.mult)
        dve_fence()
        DVE.tensor_tensor(out=s_sm[:, 15:16], in0=s_sm[:, 15:16], in1=s_sm[:, 16:17],
                          op=ALU.subtract)
        dve_fence()
        DVE.tensor_scalar(out=s_sm[:, 15:16], in0=s_sm[:, 15:16], scalar1=EPS,
                          scalar2=None, op0=ALU.add)
        DVE.sem_inc(se_dve.h, 1)
        var_t = se_dve.bump()
        ACT.wait_ge(se_dve.h, var_t)
        ACT.activation(out=s_sm[:, 17:18], in_=s_sm[:, 15:16], func=AF.Sqrt)
        ACT.sem_inc(se_act.h, 1)
        sd_t = se_act.bump()
        DVE.wait_ge(se_act.h, sd_t)
        DVE.reciprocal(out=s_sm[:, 18:19], in_=s_sm[:, 17:18])
        dve_fence()
        DVE.wait_ge(sm_z1ld[parz].h, z1ld[t128])
        if last_yst:
            DVE.wait_ge(sm_yst.h, last_yst)
        DVE.tensor_scalar(out=s_yst[:, :], in0=s_z1l[parz][:, :],
                          scalar1=s_sm[:, 14:15], scalar2=s_sm[:, 18:19],
                          op0=ALU.subtract, op1=ALU.mult)
        DVE.tensor_scalar(out=s_ybs[:, :], in0=s_z1l[parz][:, :],
                          scalar1=s_sm[:, 14:15], scalar2=s_sm[:, 18:19],
                          op0=ALU.subtract, op1=ALU.mult)
        DVE.sem_inc(se_dve.h, 1)
        y_t = se_dve.bump()
        if t128 + 2 < 16:
            SP.wait_ge(se_dve.h, y_t)
            z1ld[t128 + 2] = dma(SP, s_z1l[parz][:, :],
                                 d_z1[(t128 + 2) * 128:(t128 + 3) * 128, :], sm_z1ld[parz])
        SP.wait_ge(se_dve.h, y_t)
        dma(SP, d_yf[t128 * 128:(t128 + 1) * 128, :], s_yst[:, :], sm_yst)
        last_yst = dma(SP, d_yb[t128 * 128:(t128 + 1) * 128, :], s_ybs[:, :], sm_yst)

    plan.barrier()
    c2.close()

    dph = Scope(nc)
    s_yTb = dph.sb("s_yTb", [128, 32 * T], BF16)
    s_gsl = [dph.sb("s_gsl0", [128, 2 * 32 * 256], BF16),
             dph.sb("s_gsl1", [128, 2 * 32 * 256], BF16)]
    s_sil = dph.sb("s_sil", [128, T], BF16)
    s_has = dph.sb("s_has", [128, T], BF16)

    SP.wait_ge(sm_yst.h, sm_yst.count)
    SP.wait_ge(sm_cc1.h, 1)
    if dbg:
        for r8 in range(8):
            dma(SP, d_dbg1[r8 * HSL:(r8 + 1) * HSL, :],
                d_dq1g[r8 * HSL:(r8 + 1) * HSL, :], sm_out)
        dma(SP, d_dbgs[:, :], d_dq1s[:, :], sm_out)
    for hc in range(32):
        dma(SP, s_yTb[:, hc * T:(hc + 1) * T],
            d_yb[0:T, hc * 128:(hc + 1) * 128], sm_yt, transpose=True)
    yt_t = sm_yt.count

    def yT(hc, tc):
        return s_yTb[:, hc * T + tc * 512: hc * T + tc * 512 + 512]

    def gsl_tile(par, w, hc, su2):
        base = w * 32 * 256 + hc * 256 + su2 * 128
        return s_gsl[par][:, base:base + 128]

    s_dqiD = [dph.sb("s_dqiD0", [128, 512], F32), dph.sb("s_dqiD1", [128, 512], F32)]
    s_dqaD = dph.sb("s_dqaD", [128, 512], BF16)
    s_dqtD = dph.sb("s_dqtD", [128, 512], BF16)
    s_dqmD = dph.sb("s_dqmD", [128, 16], F32)
    s_dqmrD = dph.sb("s_dqmrD", [128, 16], F32)
    dq_bufs_d = (s_dqiD, s_dqaD, s_dqtD, s_dqmD, s_dqmrD)

    blocks = [(k, r) for k in range(6) for r in range(8)]

    def load_block(b, par):
        k, r = blocks[b]
        sz = 256 if k < 5 else 128
        SP.wait_ge(sm_cc2[k].h, 1)
        for w in range(2):
            for hc in range(32):
                if w == 0:
                    srcb = d_dq1g[1408 * r + 256 * k:1408 * r + 256 * k + sz,
                                  hc * 128:(hc + 1) * 128]
                else:
                    srcb = d_dq2gc[k][sz * r:sz * (r + 1), hc * 128:(hc + 1) * 128]
                dma(SP, s_gsl[par][:, w * 32 * 256 + hc * 256:
                                   w * 32 * 256 + hc * 256 + sz],
                    srcb, sm_gw[par], transpose=True)
        return sm_gw[par].count

    gw_t = {0: load_block(0, 0)}
    if dbg:
        SP.wait_ge(sm_yt.h, sm_yt.count)
        dma(SP, d_dbgy[:, :], s_yTb[:, :], sm_out)
    gsl_last_use = [0, 0]
    last_hast = 0
    n_su2 = 0
    for b in range(len(blocks)):
        k, r = blocks[b]
        par = b % 2
        if b + 1 < len(blocks):
            if gsl_last_use[1 - par]:
                SP.wait_ge(se_pe.h, gsl_last_use[1 - par])
            gw_t[b + 1] = load_block(b + 1, 1 - par)
        nsu = 2 if k < 5 else 1
        for su2 in range(nsu):
            Hc = 11 * r + 2 * k + su2
            TEN.wait_ge(sm_gw[par].h, gw_t[b])
            if b == 0 and su2 == 0:
                TEN.wait_ge(sm_yt.h, yt_t)
            for w in range(2):
                for tc in range(4):
                    pe_bank(w * 4 + tc)
                for hc in range(32):
                    for tc in range(4):
                        TEN.matmul(pb[w * 4 + tc][:, :], lhsT=gsl_tile(par, w, hc, su2),
                                   rhs=yT(hc, tc), start=(hc == 0), stop=(hc == 31))
                TEN.sem_inc(se_pe.h, 1)
                se_pe.bump()
            g_t = se_pe.count
            gsl_last_use[par] = g_t
            ACT.wait_ge(se_pe.h, g_t)
            for tc in range(4):
                ACT.activation(out=s_sil[:, tc * 512:(tc + 1) * 512], in_=pb[tc][:, :],
                               func=AF.Silu)
            ACT.sem_inc(se_act.h, 1)
            sil_t = se_act.bump()
            for tc in range(4):
                consumed(tc, se_act, sil_t)
            DVE.wait_ge(se_act.h, sil_t)
            if last_hast:
                DVE.wait_ge(sm_hast.h, last_hast)
            for tc in range(4):
                DVE.tensor_tensor(out=s_has[:, tc * 512:(tc + 1) * 512],
                                  in0=s_sil[:, tc * 512:(tc + 1) * 512],
                                  in1=pb[4 + tc][:, :], op=ALU.mult)
            DVE.sem_inc(se_dve.h, 1)
            ha_t = se_dve.bump()
            for tc in range(4):
                consumed(4 + tc, se_dve, ha_t)
            SP.wait_ge(se_dve.h, ha_t)
            last_hast = dma(SP, d_hact[Hc], s_has[:, :], sm_hast)
            n_su2 += 1
            if dq_state["next"] < len(dq_jobs):
                emit_dq(1, dq_bufs_d)
                issue_w2_gathers()

    plan.barrier()
    dph.close()

    e = Scope(nc)
    s_w3b = [e.sb("s_w3b0", [128, NHC * 512], BF16),
             e.sb("s_w3b1", [128, NHC * 512], BF16)]
    s_hab = [e.sb("s_hab0", [128, 8 * 512], BF16), e.sb("s_hab1", [128, 8 * 512], BF16)]
    s_yfb = [e.sb("s_yfb%d" % i, [128, 512], F32) for i in range(4)]
    s_z2s = [e.sb("s_z2s0", [128, 512], F32), e.sb("s_z2s1", [128, 512], F32)]

    SP.wait_ge(sm_hast.h, sm_hast.count)
    ACT.wait_ge(sm_hast.h, sm_hast.count)
    PL.wait_ge(sm_yst.h, sm_yst.count)
    SP.wait_ge(sm_cc3.h, 1)
    NB = NHC // 8

    def load_w3(e8, wpar):
        for Hc in range(NHC):
            r, lc = divmod(Hc, HSL // 128)
            dma(SP, s_w3b[wpar][:, Hc * 512:(Hc + 1) * 512],
                d_dq3g[r * D + e8 * 512: r * D + (e8 + 1) * 512, lc * 128:(lc + 1) * 128],
                sm_w3b[wpar], transpose=True)
        return sm_w3b[wpar].count

    w3t = {0: load_w3(0, 0)}
    w3_last_use = [0, 0]
    last_yfl = {0: 0, 1: 0, 2: 0, 3: 0}
    last_z2st = {0: 0, 1: 0}
    hab_use = [0, 0]
    for e8 in range(8):
        wpar = e8 % 2
        if e8 + 1 < 8:
            if w3_last_use[1 - wpar]:
                SP.wait_ge(se_pe.h, w3_last_use[1 - wpar])
            w3t[e8 + 1] = load_w3(e8 + 1, 1 - wpar)
        for t4 in range(4):
            yld = {}
            for sub in range(4):
                t128 = t4 * 4 + sub
                if last_yfl[sub]:
                    PL.wait_ge(se_dve.h, last_yfl[sub])
                yld[sub] = dma(PL, s_yfb[sub][:, :],
                               d_yf[t128 * 128:(t128 + 1) * 128,
                                    e8 * 512:(e8 + 1) * 512], sm_yf[sub % 2])
            ha_tgt = {}
            for hb in range(2):
                if hab_use[hb]:
                    ACT.wait_ge(se_pe.h, hab_use[hb])
                src_ap = d_hact[hb * 8:(hb + 1) * 8].rearrange("c p t -> p c t")[
                    :, :, t4 * 512:(t4 + 1) * 512]
                dst_ap = s_hab[hb].rearrange("p (c t) -> p c t", t=512)
                ha_tgt[hb] = dma(ACT, dst_ap, src_ap, sm_ha[hb])
            for sub in range(4):
                pe_bank(sub)
            TEN.wait_ge(sm_w3b[wpar].h, w3t[e8])
            for hb in range(NB):
                hpar = hb % 2
                TEN.wait_ge(sm_ha[hpar].h, ha_tgt[hb])
                for hcl in range(8):
                    Hc = hb * 8 + hcl
                    for sub in range(4):
                        TEN.matmul(pb[sub][:, :],
                                   lhsT=s_hab[hpar][:, hcl * 512 + sub * 128:
                                                    hcl * 512 + (sub + 1) * 128],
                                   rhs=s_w3b[wpar][:, Hc * 512:(Hc + 1) * 512],
                                   start=(Hc == 0), stop=(Hc == NHC - 1))
                TEN.sem_inc(se_pe.h, 1)
                bt = se_pe.bump()
                hab_use[hpar] = bt
                if hb + 2 < NB:
                    ACT.wait_ge(se_pe.h, bt)
                    src_ap = d_hact[(hb + 2) * 8:(hb + 3) * 8].rearrange(
                        "c p t -> p c t")[:, :, t4 * 512:(t4 + 1) * 512]
                    dst_ap = s_hab[hpar].rearrange("p (c t) -> p c t", t=512)
                    ha_tgt[hb + 2] = dma(ACT, dst_ap, src_ap, sm_ha[hpar])
            dn_t = se_pe.count
            w3_last_use[wpar] = dn_t
            for sub in range(4):
                t128 = t4 * 4 + sub
                zpar = sub % 2
                DVE.wait_ge(se_pe.h, dn_t)
                DVE.wait_ge(sm_yf[sub % 2].h, yld[sub])
                if last_z2st[zpar]:
                    DVE.wait_ge(sm_z2st.h, last_z2st[zpar])
                DVE.tensor_tensor(out=s_z2s[zpar][:, :], in0=pb[sub][:, :],
                                  in1=s_yfb[sub][:, :], op=ALU.add)
                DVE.sem_inc(se_dve.h, 1)
                z2_t = se_dve.bump()
                last_yfl[sub] = z2_t
                consumed(sub, se_dve, z2_t)
                PL.wait_ge(se_dve.h, z2_t)
                last_z2st[zpar] = dma(PL, d_z2[t128 * 128:(t128 + 1) * 128,
                                              e8 * 512:(e8 + 1) * 512],
                                      s_z2s[zpar][:, :], sm_z2st)

    plan.barrier()
    e.close()

    f = Scope(nc)
    s_z2l = [f.sb("s_z2l0", [128, D], F32), f.sb("s_z2l1", [128, D], F32)]
    s_os = f.sb("s_os", [128, D], F32)

    SP.wait_ge(sm_z2st.h, sm_z2st.count)
    z2ld = {}
    for t128 in range(2):
        z2ld[t128] = dma(SP, s_z2l[t128 % 2][:, :],
                         d_z2[t128 * 128:(t128 + 1) * 128, :], sm_z2ld[t128 % 2])
    last_ost = 0
    for t128 in range(16):
        parz = t128 % 2
        DVE.wait_ge(sm_z2ld[parz].h, z2ld[t128])
        DVE.tensor_reduce(out=s_sm[:, 20:21], in_=s_z2l[parz][:, :], axis=AX.X, op=ALU.add)
        dve_fence()
        DVE.tensor_scalar(out=s_sm[:, 20:21], in0=s_sm[:, 20:21], scalar1=float(1.0 / D),
                          scalar2=None, op0=ALU.mult)
        DVE.sem_inc(se_dve.h, 1)
        mn_t = se_dve.bump()
        ACT.wait_ge(sm_z2ld[parz].h, z2ld[t128])
        ACT.wait_ge(se_dve.h, mn_t)
        if last_ost:
            ACT.wait_ge(sm_out.h, last_ost)
        ACT.activation(out=s_os[:, :], in_=s_z2l[parz][:, :], func=AF.Square,
                       bias=s_sm[:, 20:21], scale=-1.0, accum_out=s_sm[:, 21:22])
        ACT.sem_inc(se_act.h, 1)
        sq_t = se_act.bump()
        DVE.wait_ge(se_act.h, sq_t)
        DVE.tensor_scalar(out=s_sm[:, 22:23], in0=s_sm[:, 21:22], scalar1=float(1.0 / D),
                          scalar2=EPS, op0=ALU.mult, op1=ALU.add)
        DVE.sem_inc(se_dve.h, 1)
        vr_t = se_dve.bump()
        ACT.wait_ge(se_dve.h, vr_t)
        ACT.activation(out=s_sm[:, 23:24], in_=s_sm[:, 22:23], func=AF.Sqrt)
        ACT.sem_inc(se_act.h, 1)
        sd_t = se_act.bump()
        DVE.wait_ge(se_act.h, sd_t)
        DVE.reciprocal(out=s_sm[:, 24:25], in_=s_sm[:, 23:24])
        dve_fence()
        DVE.tensor_scalar(out=s_os[:, :], in0=s_z2l[parz][:, :],
                          scalar1=s_sm[:, 20:21], scalar2=s_sm[:, 24:25],
                          op0=ALU.subtract, op1=ALU.mult)
        DVE.sem_inc(se_dve.h, 1)
        o_t = se_dve.bump()
        if t128 + 2 < 16:
            SP.wait_ge(se_dve.h, o_t)
            z2ld[t128 + 2] = dma(SP, s_z2l[parz][:, :],
                                 d_z2[(t128 + 2) * 128:(t128 + 3) * 128, :], sm_z2ld[parz])
        SP.wait_ge(se_dve.h, o_t)
        last_ost = dma(SP, d_out[t128 * 128:(t128 + 1) * 128, :], s_os[:, :], sm_out)

    SP.wait_ge(sm_out.h, sm_out.count)

    with nc.Block() as block:
        @block.tensor
        def _(eng):
            plan.replay("tensor", eng)

        @block.vector
        def _(eng):
            plan.replay("vector", eng)

        @block.scalar
        def _(eng):
            plan.replay("scalar", eng)

        @block.gpsimd
        def _(eng):
            plan.replay("gpsimd", eng)

        @block.sync
        def _(eng):
            plan.replay("sync", eng)

    f.close()
    for cm in reversed(pbc):
        cm.__exit__(None, None, None)
    perm.close()
    return nc


_CACHE = {}


def kernel(L0_attn_x, L0_attn_Wq, L0_attn_Wk, L0_attn_Wv, L0_attn_Wo,
           L0_ffn_w1, L0_ffn_w2, L0_ffn_w3):
    x = np.asarray(L0_attn_x, dtype=np.float32)
    Wq = np.asarray(L0_attn_Wq, dtype=np.float32)
    Wk = np.asarray(L0_attn_Wk, dtype=np.float32)
    Wv = np.asarray(L0_attn_Wv, dtype=np.float32)
    Wo = np.asarray(L0_attn_Wo, dtype=np.float32)
    w1 = np.asarray(L0_ffn_w1, dtype=np.float32)
    w2 = np.asarray(L0_ffn_w2, dtype=np.float32)
    w3 = np.asarray(L0_ffn_w3, dtype=np.float32)

    if "nc" not in _CACHE:
        _CACHE["nc"] = build_program()
    nc = _CACHE["nc"]

    cosT, sinTs, masks_bf = make_tables()
    wqkv = pack_qkv(Wq, Wk, Wv)
    woT = np.ascontiguousarray(Wo.T)
    identf = np.eye(128, dtype=np.float32)

    w1p = np.zeros((HFFP, D), dtype=np.float32)
    w1p[:HFF] = w1
    w2p = np.zeros((HFFP, D), dtype=np.float32)
    w2p[:HFF] = w2
    w3p = np.zeros((D, HFFP), dtype=np.float32)
    w3p[:, :HFF] = w3

    in_maps = []
    for ci in range(B):
        xc = np.ascontiguousarray(x[ci])
        in_maps.append({
            "xT": np.ascontiguousarray(xc.T),
            "x": xc,
            "wqkv": wqkv,
            "woT": woT,
            "w1s": np.ascontiguousarray(w1p[ci * HSL:(ci + 1) * HSL]),
            "w2s": np.ascontiguousarray(w2p[ci * HSL:(ci + 1) * HSL]),
            "w3s": np.ascontiguousarray(w3p[:, ci * HSL:(ci + 1) * HSL]),
            "cosT": cosT,
            "sinTs": sinTs,
            "masks": masks_bf,
            "identf": identf,
        })
    res = run_bass_kernel_spmd(nc, in_maps, core_ids=list(range(B)))
    out = np.stack([np.asarray(r["out"], dtype=np.float32) for r in res.results])
    return out
